# version 05
# baseline (speedup 1.0000x reference)
import numpy as np
import concourse.bass as bass
import concourse.mybir as mybir
from concourse.bass_core_rust import IndirectOffsetOnAxis

F32 = mybir.dt.float32
BF16 = mybir.dt.bfloat16
U8 = mybir.dt.uint8
AF = mybir.ActivationFunctionType
ALU = mybir.AluOpType
AX = mybir.AxisListType

D = 1024
NB = 2
T = 8192
NCORES = 8
TL = 2048

ENGINES = ("sync", "scalar", "vector", "tensor", "gpsimd")


class Event:
    __slots__ = ("sem", "value")

    def __init__(self, sem, value):
        self.sem = sem
        self.value = value


class FW:
    __slots__ = ("ev",)

    def __init__(self, ev):
        self.ev = ev


class SemT:
    __slots__ = ("name", "h", "n", "producer")

    def __init__(self, name, h):
        self.name = name
        self.h = h
        self.n = 0
        self.producer = None


class Rec:

    def __init__(self, nc):
        self.nc = nc
        self.ops = {e: [] for e in ENGINES}
        self.sems = {}
        self._guards = []
        self._scope = None
        self.last_ev = {}

    def sem(self, name):
        if name not in self.sems:
            g = self.nc.semaphore(name)
            h = g.__enter__()
            self._guards.append(g)
            self.sems[name] = SemT(name, h)
        return self.sems[name]

    def sbuf(self, name, shape, dtype):
        g = self.nc.sbuf_tensor(name, shape, dtype, allow_name_mangling=True)
        t = g.__enter__()
        (self._scope if self._scope is not None else self._guards).append(g)
        return t

    def psum(self, name, shape, dtype=F32):
        g = self.nc.psum_tensor(name, shape, dtype, allow_name_mangling=True)
        t = g.__enter__()
        (self._scope if self._scope is not None else self._guards).append(g)
        return t

    def scope_begin(self):
        assert self._scope is None
        self._scope = []

    def scope_end(self):
        for g in reversed(self._scope):
            g.__exit__(None, None, None)
        self._scope = None

    def barrier(self):
        evs = list(self.last_ev.values())
        for e in ENGINES:
            self.wait(e, *evs)

    def op(self, engine, fn, waits=(), inc=None, inc_k=1, dma=False):
        assert engine in ENGINES
        ev = None
        if inc is not None:
            s = self.sem(inc) if isinstance(inc, str) else inc
            if s.producer is None:
                s.producer = engine
            assert s.producer == engine, (s.name, s.producer, engine)
            s.n += inc_k
            ev = Event(s, s.n)
        self.ops[engine].append((fn, tuple(w for w in waits if w is not None),
                                 ev, inc_k, dma))
        if ev is not None:
            self.last_ev[ev.sem.name] = ev
        return ev

    def wait(self, engine, *events):
        self.ops[engine].append((None, tuple(w for w in events if w is not None),
                                 None, 0, False))

    def dma(self, queue, out, in_, waits=(), inc=None, **kw):
        return self.op(queue, lambda e: e.dma_start(out=out, in_=in_, **kw),
                       waits=waits, inc=inc, inc_k=16, dma=True)

    def emit(self):
        with self.nc.Block() as block:
            for eng_name in ENGINES:
                ops = self.ops[eng_name]
                if not ops:
                    continue

                def section(eng, _ops=ops, eng_name=eng_name):
                    watermark = {}
                    for fn, waits, ev, inc_k, dma in _ops:
                        seen = set()
                        for w in waits:
                            if isinstance(w, FW):
                                w = w.ev
                            if (w.sem.name, w.value) in seen:
                                continue
                            if watermark.get(w.sem.name, -1) >= w.value:
                                continue
                            seen.add((w.sem.name, w.value))
                            eng.wait_ge(w.sem.h, w.value)
                            watermark[w.sem.name] = max(
                                watermark.get(w.sem.name, 0), w.value)
                        if fn is None:
                            continue
                        inst = fn(eng)
                        if ev is not None:
                            inst.then_inc(ev.sem.h, inc_k)
                            if not dma and eng_name == "tensor":
                                watermark[ev.sem.name] = ev.value

                getattr(block, eng_name)(section)
        return


def ceil_div(a, b):
    return (a + b - 1) // b


def _fl(x):
    if x is None:
        return []
    if isinstance(x, (list, tuple)):
        return [e for e in x if e is not None]
    return [x]



def blockify_lhsT(w, kchunk=128, fchunk=128):
    K, F = w.shape
    nch, nf = K // kchunk, F // fchunk
    return np.ascontiguousarray(
        w.reshape(nch, kchunk, nf, fchunk).transpose(2, 1, 0, 3))



def make_id128(r, dtype=BF16, name="id128"):
    idn = r.sbuf(name, [128, 128], dtype)
    e0 = r.op("gpsimd", lambda e: e.memset(idn[:, :], 0), inc="g_const")
    ev = r.op("gpsimd", lambda e: e.affine_select(
        out=idn[:, :], in_=idn[:, :], compare_op=ALU.not_equal,
        fill=1.0, base=0, pattern=[[-1, 128]], channel_multiplier=1),
        waits=[e0], inc="g_const")
    return idn, ev


class LN:

    def __init__(self, r, name, ntiles, eps=1e-5):
        self.r = r
        self.eps = eps
        self.st = r.sbuf(f"{name}_st", [128, ntiles * 8], F32)
        self.sq = r.sbuf(f"{name}_sq", [128, D], BF16)
        self.ntiles = ntiles
        self.sv = r.sem(f"{name}_v")
        self.sa = r.sem(f"{name}_a")
        self._slot_act = [None] * ntiles

    def tile(self, idx, x_tile, xn_out, x_ready=()):
        r = self.r
        c = (idx % self.ntiles) * 8
        st = self.st
        s1 = st[:, c + 0:c + 1]
        nm = st[:, c + 1:c + 2]
        s2 = st[:, c + 2:c + 3]
        var = st[:, c + 3:c + 4]
        std = st[:, c + 4:c + 5]
        rst = st[:, c + 5:c + 6]
        bia = st[:, c + 6:c + 7]
        slot = idx % self.ntiles
        e1 = r.op("vector", lambda e: e.reduce_sum(s1, x_tile, axis=AX.X),
                  waits=tuple(x_ready) + ((self._slot_act[slot],)
                                          if self._slot_act[slot] else ()),
                  inc=self.sv)
        e2 = r.op("vector", lambda e: e.tensor_scalar_mul(nm, s1, -1.0 / D),
                  waits=[FW(e1)], inc=self.sv)
        e3 = r.op("scalar", lambda e: e.activation(
            out=self.sq[:, :], in_=x_tile, func=AF.Square,
            bias=nm, scale=1.0, accum_out=s2),
            waits=[e2], inc=self.sa)
        e4 = r.op("vector", lambda e: e.tensor_scalar(
            out=var, in0=s2, scalar1=1.0 / D, scalar2=self.eps,
            op0=ALU.mult, op1=ALU.add), waits=[e3], inc=self.sv)
        e5 = r.op("scalar", lambda e: e.activation(
            out=std, in_=var, func=AF.Sqrt),
            waits=[e4], inc=self.sa)
        e6 = r.op("vector", lambda e: e.reciprocal(rst, std), waits=[e5],
                  inc=self.sv)
        e7 = r.op("vector", lambda e: e.tensor_mul(bia, nm, rst),
                  waits=[FW(e6)], inc=self.sv)
        e8 = r.op("vector", lambda e: e.tensor_scalar(
            out=xn_out, in0=x_tile, scalar1=rst, scalar2=bia,
            op0=ALU.mult, op1=ALU.add), waits=[FW(e7)], inc=self.sv)
        self._slot_act[slot] = e5
        return e8



def emit_bcast_scalars(r, svec_dram, n_s, scales, sbc, ones1, svt, pbank, tag):
    ld = r.dma("sync", out=svt[:, 0:n_s], in_=svec_dram, inc=f"{tag}_dsv")
    on = r.op("gpsimd", lambda e: e.memset(ones1[:, :], 1.0), inc="g")
    mm = r.op("tensor", lambda e: e.matmul(pbank[:, 0:n_s], ones1[:, :],
                                           svt[:, 0:n_s], start=True, stop=True),
              waits=[ld, on], inc="t")
    ev = None
    for i, sc in enumerate(scales):
        ev = r.op("vector", lambda e, i=i, sc=sc: e.tensor_scalar_mul(
            sbc[:, i:i + 1], pbank[:, i:i + 1], float(sc)), waits=[mm], inc="v")
    return ev


def emit_s2(r, xt, x_ready, k0b, k1b, k2r, svec, tag="s2",
            idn=None, idn_ev=None, nsub=4, debug=None):
    NSUB = nsub
    SUBT = 4

    xn_tm = r.sbuf(tag + "_xn", [128, SUBT * D], BF16)
    xn_fm = r.sbuf(tag + "_xnf", [128, 8 * 512], BF16)
    h0 = r.sbuf(tag + "_h0", [128, 32 * 512], BF16)
    h1 = r.sbuf(tag + "_h1", [128, 32 * 512], BF16)
    k0s = [r.sbuf(tag + f"_k0s{p}", [128, 16 * 128], BF16) for p in range(2)]
    d0w = [r.sbuf(tag + f"_d0w{p}", [128, 8 * 128], BF16) for p in range(2)]
    k1s = [r.sbuf(tag + f"_k1s{p}", [128, 32 * 128], BF16) for p in range(2)]
    w1g = [r.sbuf(tag + f"_w1g{p}", [128, 32 * 128], BF16) for p in range(2)]
    k2s = [r.sbuf(tag + f"_k2s{p}", [128, 1024], BF16) for p in range(2)]
    w2c = [r.sbuf(tag + f"_w2c{p}", [128, 1024], BF16) for p in range(2)]
    sbc = r.sbuf(tag + "_sbc", [128, 4], F32)
    ones1 = r.sbuf(tag + "_on", [1, 128], F32)
    svt = r.sbuf(tag + "_sv", [1, 4], F32)
    pb = r.psum(tag + "_pb", [128, 4096], F32)
    pbank = lambda b: pb[:, b * 512:(b + 1) * 512]
    ptr = pb.bitcast(BF16)
    tslot = lambda b: ptr[:, b * 1024:b * 1024 + 128]

    if idn is None:
        idn, idn_ev = make_id128(r, BF16, tag + "_id")

    ln = LN(r, tag + "_ln", SUBT)
    sbc_ev = emit_bcast_scalars(r, svec[:, 0:3], 3, [4.0, 2.0, 1.0],
                                sbc, ones1, svt, pbank(5), tag)

    bank_free = {b: None for b in range(8)}
    bank_free[5] = sbc_ev
    sem_k0 = [r.sem(tag + f"_k0{p}") for p in range(2)]
    sem_k1 = [r.sem(tag + f"_k1{p}") for p in range(2)]
    sem_k2 = [r.sem(tag + f"_k2{p}") for p in range(2)]
    k0s_free = [None, None]; d0w_free = [None, None]
    k1s_free = [None, None]; w1g_free = [None, None]
    k2s_free = [None, None]; w2c_free = [None, None]
    h0_free = None
    h1_free = None
    xn_tm_free = None
    xn_fm_free = None
    last_evac = None

    for sub in range(NSUB):
        ln_evs = []
        for t in range(SUBT):
            gt = sub * SUBT + t
            ev = ln.tile(t, xt[:, gt * D:(gt + 1) * D],
                         xn_tm[:, t * D:(t + 1) * D],
                         x_ready=[x_ready, xn_tm_free])
            ln_evs.append(ev)
        tr_evs = []
        xn_ready = None
        for t in range(SUBT):
            for c in range(8):
                b = 6 + ((t * 8 + c) % 2)
                tr = r.op("tensor", lambda e, b=b, t=t, c=c: e.transpose(
                    tslot(b), xn_tm[:, t * D + c * 128:t * D + (c + 1) * 128],
                    idn[:, :]),
                    waits=[ln_evs[t], idn_ev, bank_free[b], h1_free], inc="t")
                ev = r.op("scalar", lambda e, b=b, t=t, c=c: e.copy(
                    xn_fm[:, c * 512 + t * 128:c * 512 + (t + 1) * 128],
                    tslot(b)), waits=[tr, xn_fm_free], inc="a")
                bank_free[b] = ev
                tr_evs.append(tr)
                xn_ready = ev
        xn_tm_free = tr_evs[-1]

        m0_last_mm = None
        for f0 in range(32):
            sl = f0 % 2
            ld = r.dma("gpsimd", out=k0s[sl][:, :],
                       in_=k0b[f0].rearrange("p c j -> p (c j)"),
                       waits=[k0s_free[sl]], inc=sem_k0[sl])
            cv = r.op("vector", lambda e, sl=sl: e.tensor_sub(
                d0w[sl][:, :], k0s[sl][:, 0:1024], k0s[sl][:, 1024:2048]),
                waits=[ld, d0w_free[sl]], inc="v")
            k0s_free[sl] = cv
            b = f0 % 2
            mm = None
            for c in range(8):
                w0 = [bank_free[b], cv, xn_ready] if c == 0 else []
                mm = r.op("tensor", lambda e, b=b, sl=sl, c=c: e.matmul(
                    pbank(b), d0w[sl][:, c * 128:(c + 1) * 128],
                    xn_fm[:, c * 512:(c + 1) * 512],
                    start=(c == 0), stop=(c == 7)),
                    waits=w0, inc="t" if c == 7 else None)
            d0w_free[sl] = mm
            m0_last_mm = mm
            ev = r.op("vector", lambda e, b=b, f0=f0: e.tensor_scalar(
                out=h0[:, f0 * 512:(f0 + 1) * 512], in0=pbank(b),
                scalar1=sbc[:, 0:1], scalar2=0.0, op0=ALU.mult, op1=ALU.max),
                waits=[mm, sbc_ev, h0_free], inc="v")
            bank_free[b] = ev
            h0_ready = ev
        xn_fm_free = m0_last_mm

        m1_last_mm = None
        for g in range(32):
            sl = g % 2
            ld = r.dma("gpsimd", out=k1s[sl][:, :],
                       in_=k1b[g].rearrange("p c j -> p (c j)"),
                       waits=[k1s_free[sl]], inc=sem_k1[sl])
            cv = r.op("vector", lambda e, sl=sl: e.tensor_scalar(
                out=w1g[sl][:, :], in0=k1s[sl][:, :], scalar1=2.0,
                scalar2=-1.0, op0=ALU.mult, op1=ALU.add),
                waits=[ld, w1g_free[sl]], inc="v")
            k1s_free[sl] = cv
            b = 2 + g % 2
            mm = None
            for f in range(32):
                w0 = [bank_free[b], cv, h0_ready] if f == 0 else []
                mm = r.op("tensor", lambda e, b=b, sl=sl, f=f: e.matmul(
                    pbank(b), w1g[sl][:, f * 128:(f + 1) * 128],
                    h0[:, f * 512:(f + 1) * 512],
                    start=(f == 0), stop=(f == 31)),
                    waits=w0, inc="t" if f == 31 else None)
            w1g_free[sl] = mm
            m1_last_mm = mm
            ev = r.op("vector", lambda e, b=b, g=g: e.tensor_scalar(
                out=h1[:, g * 512:(g + 1) * 512], in0=pbank(b),
                scalar1=sbc[:, 1:2], scalar2=0.0, op0=ALU.mult, op1=ALU.max),
                waits=[mm, h1_free], inc="v")
            bank_free[b] = ev
            h1_ready = ev
        h0_free = m1_last_mm

        m2_last_mm = None
        for pas in range(2):
            mm_done = {}
            for g in range(32):
                sl = g % 2
                ld = r.dma("gpsimd", out=k2s[sl][:, :],
                           in_=k2r[g * 128:(g + 1) * 128, :],
                           waits=[k2s_free[sl]], inc=sem_k2[sl])
                cv = r.op("vector", lambda e, sl=sl: e.tensor_scalar(
                    out=w2c[sl][:, :], in0=k2s[sl][:, :], scalar1=2.0,
                    scalar2=-1.0, op0=ALU.mult, op1=ALU.add),
                    waits=[ld, w2c_free[sl]], inc="v")
                k2s_free[sl] = cv
                mm = None
                for tt in range(2):
                    for hf in range(2):
                        b = [0, 1, 4, 5][tt * 2 + hf]
                        tok = pas * 2 + tt
                        last_of_g = (tt == 1 and hf == 1)
                        w0 = [cv]
                        if g == 0:
                            w0 += [bank_free[b], h1_ready]
                        mm = r.op("tensor",
                                  lambda e, b=b, sl=sl, g=g, tok=tok, hf=hf:
                                  e.matmul(
                                      pbank(b),
                                      h1[:, g * 512 + tok * 128:
                                         g * 512 + (tok + 1) * 128],
                                      w2c[sl][:, hf * 512:(hf + 1) * 512],
                                      start=(g == 0), stop=(g == 31)),
                                  waits=w0,
                                  inc="t" if (g == 31 or last_of_g) else None)
                        if g == 31:
                            mm_done[(tt, hf)] = mm
                w2c_free[sl] = mm
                m2_last_mm = mm
            for tt in range(2):
                for hf in range(2):
                    b = [0, 1, 4, 5][tt * 2 + hf]
                    gt = sub * SUBT + pas * 2 + tt
                    xsl = xt[:, gt * D + hf * 512:gt * D + (hf + 1) * 512]
                    ev = r.op("vector", lambda e, b=b, xsl=xsl:
                              e.scalar_tensor_tensor(
                                  out=xsl, in0=pbank(b), scalar=sbc[:, 2:3],
                                  in1=xsl, op0=ALU.mult, op1=ALU.add),
                              waits=[mm_done[(tt, hf)]], inc="v")
                    bank_free[b] = ev
                    last_evac = ev
        h1_free = m2_last_mm
    if debug is not None:
        debug.update(xn_tm=xn_tm, xn_fm=xn_fm, h0=h0, h1=h1, last=last_evac)
    return last_evac


class SubPrep:

    def __init__(self, r, tag, ln, idn, idn_ev, pb, bank_free, subt=4):
        self.r = r
        self.tag = tag
        self.ln = ln
        self.idn = idn
        self.idn_ev = idn_ev
        self.pb = pb
        self.bank_free = bank_free
        self.subt = subt
        self.xn_tm = r.sbuf(tag + "_xn", [128, subt * D], BF16)
        self.xn_fm = r.sbuf(tag + "_xnf", [128, 8 * subt * 128], BF16)
        self.xn_tm_free = None
        self.xn_fm_free = None
        self.ptr = pb.bitcast(BF16)

    def emit(self, xt, sub, x_ready, tbanks=(6, 7)):
        r = self.r
        ln_evs = []
        for t in range(self.subt):
            gt = sub * self.subt + t
            ev = self.ln.tile(t, xt[:, gt * D:(gt + 1) * D],
                              self.xn_tm[:, t * D:(t + 1) * D],
                              x_ready=[x_ready, self.xn_tm_free])
            ln_evs.append(ev)
        xn_ready = None
        last_tr = None
        for t in range(self.subt):
            for c in range(8):
                b = tbanks[(t * 8 + c) % len(tbanks)]
                tslot = self.ptr[:, b * 1024:b * 1024 + 128]
                tr = r.op("tensor", lambda e, tslot=tslot, t=t, c=c: e.transpose(
                    tslot, self.xn_tm[:, t * D + c * 128:t * D + (c + 1) * 128],
                    self.idn[:, :]),
                    waits=[ln_evs[t], self.idn_ev, self.bank_free[b]], inc="t")
                ev = r.op("scalar", lambda e, tslot=tslot, t=t, c=c: e.copy(
                    self.xn_fm[:, c * self.subt * 128 + t * 128:
                               c * self.subt * 128 + (t + 1) * 128],
                    tslot), waits=[tr, self.xn_fm_free], inc="a")
                self.bank_free[b] = ev
                xn_ready = ev
                last_tr = tr
        self.xn_tm_free = last_tr
        return xn_ready

    def fm(self, c, t0=0, nt=None):
        nt = self.subt if nt is None else nt
        w = self.subt * 128
        return self.xn_fm[:, c * w + t0 * 128:c * w + (t0 + nt) * 128]


def make_ltri8(r, tag):
    lt = r.sbuf(tag + "_ltri", [8, 8], F32)
    e0 = r.op("gpsimd", lambda e: e.memset(lt[:, :], 0), inc="g_const")
    ev = r.op("gpsimd", lambda e: e.affine_select(
        out=lt[:, :], in_=lt[:, :], compare_op=ALU.is_ge,
        fill=1.0, base=0, pattern=[[-1, 8]], channel_multiplier=1),
        waits=[e0], inc="g_const")
    return lt, ev



def emit_s6(r, xt, x_ready, rwt_d, rbt_d, w1b_d, b1_d, w2_d, b2_d, tag="s6",
            idn=None, idn_ev=None, nsub=4, debug=None):
    SUBT = 4
    NE = 8

    if idn is None:
        idn, idn_ev = make_id128(r, BF16, tag + "_id")
    idnf = r.sbuf(tag + "_idf", [128, 128], F32)
    ei0 = r.op("gpsimd", lambda e: e.memset(idnf[:, :], 0), inc="g_const")
    idnf_ev = r.op("gpsimd", lambda e: e.affine_select(
        out=idnf[:, :], in_=idnf[:, :], compare_op=ALU.not_equal,
        fill=1.0, base=0, pattern=[[-1, 128]], channel_multiplier=1),
        waits=[ei0], inc="g_const")
    ltri, ltri_ev = make_ltri8(r, tag)

    pb = r.psum(tag + "_pb", [128, 4096], F32)
    pbank = lambda b, w=512: pb[:, b * 512:b * 512 + w]
    bank_free = {b: None for b in range(8)}
    ln = LN(r, tag + "_ln", SUBT)
    sp = SubPrep(r, tag, ln, idn, idn_ev, pb, bank_free, SUBT)

    rwf = r.sbuf(tag + "_rwf", [128, 8 * 8], F32)
    rbb = r.sbuf(tag + "_rbb", [128, 8], F32)
    rbt = r.sbuf(tag + "_rbt", [1, 8], F32)
    csr = r.sbuf(tag + "_csr", [1, 8], F32)
    csb = r.sbuf(tag + "_csb", [128, 8], F32)
    b1t = r.sbuf(tag + "_b1", [128, NE * 32], F32)
    b2t = r.sbuf(tag + "_b2", [1, NE * 1024], BF16)
    onesb = r.sbuf(tag + "_onb", [1, 128], BF16)
    onesf = r.sbuf(tag + "_onf", [1, 128], F32)
    e_rw = r.dma("sync", out=rwf[:, :].rearrange("p (c e) -> p c e", e=8),
                 in_=rwt_d.rearrange("(c p) e -> p c e", p=128), inc="dS")
    e_rb = r.dma("sync", out=rbt[:, :], in_=rbt_d, inc="dS")
    e_b1 = r.dma("sync", out=b1t[:, :], in_=b1_d, inc="dS")
    e_b2 = r.dma("gpsimd", out=b2t[:, :], in_=b2_d, inc="dG")
    eS = e_b1
    onescol = r.sbuf(tag + "_onc", [128, 1], F32)
    ev_onb = r.op("gpsimd", lambda e: e.memset(onesb[:, :], 1.0), inc="g_const")
    ev_onf = r.op("gpsimd", lambda e: e.memset(onesf[:, :], 1.0), inc="g_const")
    ev_onc = r.op("gpsimd", lambda e: e.memset(onescol[:, :], 1.0), inc="g_const")
    mmrb = r.op("tensor", lambda e: e.matmul(pbank(5, 8), onesf[:, :],
                                             rbt[:, :], start=True, stop=True),
                waits=[eS, ev_onf], inc="t")
    ev_rbb = r.op("vector", lambda e: e.tensor_copy(rbb[:, :], pbank(5, 8)),
                  waits=[mmrb], inc="v")
    for c in range(8):
        mmcs = r.op("tensor", lambda e, c=c: e.matmul(
            pb[0:1, 5 * 512 + 16:5 * 512 + 24], onescol[:, :],
            rwf[:, c * 8:(c + 1) * 8],
            start=(c == 0), stop=(c == 7)),
            waits=[eS, ev_onc, ev_rbb], inc="t" if c == 7 else None)
    ev_csr = r.op("vector", lambda e: e.tensor_copy(
        csr[:, :], pb[0:1, 5 * 512 + 16:5 * 512 + 24]), waits=[mmcs], inc="v")
    mmcb = r.op("tensor", lambda e: e.matmul(
        pbank(5, 8), onesf[:, :], csr[:, :], start=True, stop=True),
        waits=[ev_csr], inc="t")
    ev_csb = r.op("vector", lambda e: e.tensor_copy(csb[:, :], pbank(5, 8)),
                  waits=[mmcb], inc="v")
    bank_free[5] = ev_csb

    xf_fm = r.sbuf(tag + "_xf", [128, 8 * SUBT * 128], F32)
    w1s = [r.sbuf(tag + f"_w1s{p}", [128, 4 * 1024], BF16) for p in range(2)]
    w2s = [r.sbuf(tag + f"_w2s{p}", [128, 4 * 1024], BF16) for p in range(2)]
    hbuf = r.sbuf(tag + "_h", [128, 32 * 512], BF16)
    acc = r.sbuf(tag + "_acc", [128, SUBT * D], F32)
    gts = r.sbuf(tag + "_gts", [128, SUBT * 8], F32)
    eqb = r.sbuf(tag + "_eqb", [128, SUBT * 8], F32)
    z1b = r.sbuf(tag + "_z1b", [128, SUBT * 8], F32)
    eqs = r.sbuf(tag + "_eqs", [8, SUBT * 128], F32)
    sst = r.sbuf(tag + "_sst", [128, SUBT * 8], F32)
    sem_w1 = [r.sem(tag + f"_w1{p}") for p in range(2)]
    sem_w2 = [r.sem(tag + f"_w2{p}") for p in range(2)]
    w1s_free = [None, None]
    w2s_free = [None, None]
    hbuf_free = None
    gts_free = None
    acc_res_ev = None
    last_evac = None

    xf_free = None
    for sub in range(nsub):
        xn_ready = sp.emit(xt, sub, x_ready)

        xf_evs = []
        for t in range(SUBT):
            gt_ = sub * SUBT + t
            for c in range(8):
                b = 6 + ((t * 8 + c) % 2)
                tsl = pb[:, b * 512:b * 512 + 128]
                tr = r.op("tensor", lambda e, tsl=tsl, gt_=gt_, c=c: e.transpose(
                    tsl, xt[:, gt_ * D + c * 128:gt_ * D + (c + 1) * 128],
                    idnf[:, :]),
                    waits=[x_ready, idnf_ev, bank_free[b]], inc="t")
                ev = r.op("vector", lambda e, tsl=tsl, t=t, c=c: e.tensor_copy(
                    xf_fm[:, c * SUBT * 128 + t * 128:
                          c * SUBT * 128 + (t + 1) * 128], tsl),
                    waits=[tr, xf_free], inc="v")
                bank_free[b] = ev
                xf_evs.append(ev)
        xf_ready = xf_evs[-1]

        gates_ev = None
        for t in range(SUBT):
            c0 = t * 8
            slot = t % SUBT
            rstd_ap = ln.st[:, slot * 8 + 5:slot * 8 + 6]
            bia_ap = ln.st[:, slot * 8 + 6:slot * 8 + 7]
            lgt = gts[:, c0:c0 + 8]
            nmx = sst[:, c0 + 0:c0 + 1]
            zz = sst[:, c0 + 1:c0 + 2]
            rz = sst[:, c0 + 2:c0 + 3]
            eqc = eqb[:, c0:c0 + 8]
            z1c = z1b[:, c0:c0 + 8]
            eqr = eqs[:, t * 128:(t + 1) * 128]
            mm = None
            for c in range(8):
                w0 = [eS, xf_ready, bank_free[4]] if c == 0 else []
                mm = r.op("tensor", lambda e, c=c, t=t: e.matmul(
                    pbank(4, 8),
                    xf_fm[:, c * SUBT * 128 + t * 128:
                          c * SUBT * 128 + (t + 1) * 128],
                    rwf[:, c * 8:(c + 1) * 8],
                    start=(c == 0), stop=(c == 7)),
                    waits=w0, inc="t" if c == 7 else None)
            xf_free = mm
            e1 = r.op("vector", lambda e, lgt=lgt, rstd_ap=rstd_ap:
                      e.scalar_tensor_tensor(
                          out=lgt, in0=pbank(4, 8), scalar=rstd_ap,
                          in1=rbb[:, :], op0=ALU.mult, op1=ALU.add),
                      waits=[mm, ev_rbb, gts_free], inc="v")
            e1b = r.op("vector", lambda e, lgt=lgt, bia_ap=bia_ap:
                       e.scalar_tensor_tensor(
                           out=lgt, in0=csb[:, :], scalar=bia_ap,
                           in1=lgt, op0=ALU.mult, op1=ALU.add),
                       waits=[e1, ev_csb], inc="v")
            e1 = e1b
            bank_free[4] = e1
            e2 = r.op("vector", lambda e, lgt=lgt, nmx=nmx: e.reduce_max(
                nmx, lgt, axis=AX.X, negate=True), waits=[e1], inc="v")
            e3 = r.op("scalar", lambda e, lgt=lgt, nmx=nmx, zz=zz: e.activation(
                out=lgt, in_=lgt, func=AF.Exp, bias=nmx, scale=1.0,
                accum_out=zz), waits=[e2], inc="a")
            e4 = r.op("vector", lambda e, zz=zz, rz=rz: e.reciprocal(rz, zz),
                      waits=[e3], inc="v")
            e5 = r.op("vector", lambda e, lgt=lgt, rz=rz: e.tensor_scalar_mul(
                lgt, lgt, rz), waits=[e4], inc="v")
            e6 = r.op("vector", lambda e, lgt=lgt, nmx=nmx: e.reduce_max(
                nmx, lgt, axis=AX.X), waits=[e5], inc="v")
            e7 = r.op("vector", lambda e, lgt=lgt, nmx=nmx, eqc=eqc:
                      e.tensor_scalar(out=eqc, in0=lgt, scalar1=nmx,
                                      scalar2=None, op0=ALU.is_equal),
                      waits=[e6], inc="v")
            tr8 = r.op("tensor", lambda e, eqc=eqc: e.transpose(
                pb[0:8, 5 * 512:5 * 512 + 128], eqc, idnf[:, :]),
                waits=[e7, idnf_ev, bank_free[5]], inc="t")
            ev8 = r.op("vector", lambda e, eqr=eqr: e.tensor_copy(
                eqr, pb[0:8, 5 * 512:5 * 512 + 128]), waits=[tr8], inc="v")
            bank_free[5] = ev8
            cm = r.op("tensor", lambda e, eqr=eqr: e.matmul(
                pbank(4, 8), eqr, ltri[:, :], start=True, stop=True),
                waits=[ev8, ltri_ev, bank_free[4]], inc="t")
            e9 = r.op("vector", lambda e, z1c=z1c: e.tensor_single_scalar(
                z1c, pbank(4, 8), 0.5, ALU.is_lt), waits=[cm], inc="v")
            bank_free[4] = e9
            e10 = r.op("vector", lambda e, eqc=eqc, z1c=z1c: e.tensor_mul(
                eqc, eqc, z1c), waits=[e7, e9], inc="v")
            e11 = r.op("vector", lambda e, lgt=lgt, eqc=eqc: e.tensor_mul(
                lgt, lgt, eqc), waits=[e10, e5], inc="v")
            gates_ev = e11

        for ex in range(NE):
            h_ready = None
            for q in range(8):
                sl = q % 2
                ld = r.dma("gpsimd", out=w1s[sl][:, :].rearrange(
                    "p (b c j) -> p b c j", b=4, c=8),
                    in_=w1b_d[ex, q * 4:(q + 1) * 4].rearrange(
                        "b p c j -> p b c j"),
                    waits=[w1s_free[sl]], inc=sem_w1[sl])
                for bb in range(4):
                    f0 = q * 4 + bb
                    b = f0 % 2
                    mm = None
                    for c in range(8):
                        w0 = [ld, bank_free[b], xn_ready] if c == 0 else []
                        mm = r.op("tensor", lambda e, b=b, sl=sl, bb=bb, c=c:
                                  e.matmul(
                                      pbank(b),
                                      w1s[sl][:, bb * 1024 + c * 128:
                                              bb * 1024 + (c + 1) * 128],
                                      sp.fm(c), start=(c == 0), stop=(c == 7)),
                                  waits=w0, inc="t" if c == 7 else None)
                    if bb == 3:
                        w1s_free[sl] = mm
                    ev = r.op("vector", lambda e, b=b, f0=f0, ex=ex:
                              e.tensor_scalar(
                                  out=hbuf[:, f0 * 512:(f0 + 1) * 512],
                                  in0=pbank(b),
                                  scalar1=b1t[:, ex * 32 + f0:ex * 32 + f0 + 1],
                                  scalar2=0.0, op0=ALU.add, op1=ALU.max),
                              waits=[mm, eS, hbuf_free], inc="v")
                    bank_free[b] = ev
                    h_ready = ev
            mm_done = {}
            m2_first = {}
            for q in range(8):
                sl = q % 2
                ld = r.dma("gpsimd", out=w2s[sl][:, :].rearrange(
                    "p (b d) -> p b d", b=4),
                    in_=w2_d[ex, q * 512:(q + 1) * 512, :].rearrange(
                        "(b p) d -> p b d", p=128),
                    waits=[w2s_free[sl]], inc=sem_w2[sl])
                for bb in range(4):
                    f = q * 4 + bb
                    for tt in range(SUBT):
                        for hf in range(2):
                            b = tt * 2 + hf
                            w0 = [ld]
                            if f == 0:
                                w0 += [bank_free[b], h_ready]
                            last_of_f = (tt == SUBT - 1 and hf == 1)
                            mm = r.op("tensor",
                                      lambda e, b=b, sl=sl, bb=bb, f=f, tt=tt,
                                      hf=hf: e.matmul(
                                          pbank(b),
                                          hbuf[:, f * 512 + tt * 128:
                                               f * 512 + (tt + 1) * 128],
                                          w2s[sl][:, bb * 1024 + hf * 512:
                                                  bb * 1024 + (hf + 1) * 512],
                                          start=(f == 0), stop=False),
                                      waits=w0,
                                      inc="t" if last_of_f else None)
                            if f == 31:
                                mm_done[(tt, hf)] = mm
                    if bb == 3:
                        w2s_free[sl] = mm
            hbuf_free = mm
            for tt in range(SUBT):
                for hf in range(2):
                    b = tt * 2 + hf
                    mmb = r.op("tensor", lambda e, b=b, ex=ex, hf=hf: e.matmul(
                        pbank(b), onesb[:, :],
                        b2t[:, ex * 1024 + hf * 512:ex * 1024 + (hf + 1) * 512],
                        start=False, stop=True),
                        waits=[e_b2, ev_onb, mm_done[(tt, hf)]], inc="t")
                    gcol = gts[:, tt * 8 + ex:tt * 8 + ex + 1]
                    if ex == 0:
                        ev = r.op("vector", lambda e, b=b, tt=tt, hf=hf,
                                  gcol=gcol: e.tensor_scalar(
                                      out=acc[:, tt * D + hf * 512:
                                              tt * D + (hf + 1) * 512],
                                      in0=pbank(b), scalar1=gcol, scalar2=None,
                                      op0=ALU.mult),
                                  waits=[mmb, gates_ev, acc_res_ev], inc="v")
                    else:
                        ev = r.op("vector", lambda e, b=b, tt=tt, hf=hf,
                                  gcol=gcol: e.scalar_tensor_tensor(
                                      out=acc[:, tt * D + hf * 512:
                                              tt * D + (hf + 1) * 512],
                                      in0=pbank(b), scalar=gcol,
                                      in1=acc[:, tt * D + hf * 512:
                                              tt * D + (hf + 1) * 512],
                                      op0=ALU.mult, op1=ALU.add),
                                  waits=[mmb], inc="v")
                    bank_free[b] = ev
                    acc_ev = ev
        for t in range(SUBT):
            gt_ = sub * SUBT + t
            ev = r.op("vector", lambda e, t=t, gt_=gt_: e.tensor_add(
                xt[:, gt_ * D:(gt_ + 1) * D], xt[:, gt_ * D:(gt_ + 1) * D],
                acc[:, t * D:(t + 1) * D]), waits=[acc_ev], inc="v")
            last_evac = ev
        acc_res_ev = last_evac
        gts_free = last_evac
    if debug is not None:
        debug.update(gts=gts, sst=sst, eqb=eqb, hbuf=hbuf, acc=acc)
    return last_evac


def bcast_ap(ap, n, axis=1):
    import concourse.bass as _bass
    a = [list(d) for d in ap.ap]
    return _bass.AP(ap.tensor, ap.offset, [a[0], [0, n], a[1]])


def emit_rope_neox(r, dst, src, cos_ap, sin_ap, nheads, hd=64, tmp=None,
                   waits=(), tag=""):
    hh = hd // 2
    sv = lambda a, o: a.rearrange("p (h f) -> p h f", f=hd)[:, :, o:o + hh]
    dv = lambda a, o: a.rearrange("p (h f) -> p h f", f=hd)[:, :, o:o + hh]
    tv = lambda a, i: a.rearrange("p (i h f) -> p i h f", i=2, f=hh)[:, i]
    C = bcast_ap(cos_ap, nheads)
    S = bcast_ap(sin_ap, nheads)
    x1 = sv(src, 0)
    x2 = sv(src, hh)
    t1 = tv(tmp, 0)
    t2 = tv(tmp, 1)
    e1 = r.op("vector", lambda e: e.tensor_mul(t1, x1, C), waits=waits, inc="v")
    e2 = r.op("vector", lambda e: e.tensor_mul(t2, x2, S), inc="v")
    e3 = r.op("vector", lambda e: e.tensor_sub(dv(dst, 0), t1, t2),
              waits=[e1, e2], inc="v")
    e4 = r.op("vector", lambda e: e.tensor_mul(t1, x2, C), waits=[e3], inc="v")
    e5 = r.op("vector", lambda e: e.tensor_mul(t2, x1, S), inc="v")
    e6 = r.op("vector", lambda e: e.tensor_add(dv(dst, hh), t1, t2),
              waits=[e4, e5], inc="v")
    return e6



def emit_s1(r, xt, x_ready, wqkvT_d, woT_d, bout_d, csk_d, snk_d, csq_d,
            snq_d, masks_d, tag="s1", idn=None, idn_ev=None, xts=None):
    NT = 18
    NQ = 16
    if xts is None:
        xts = lambda i: xt[:, i * 1024:(i + 1) * 1024]

    if idn is None:
        idn, idn_ev = make_id128(r, BF16, tag + "_id")
    pb = r.psum(tag + "_pb", [128, 4096], F32)
    pbank = lambda b, w=512: pb[:, b * 512:b * 512 + w]
    ptr = pb.bitcast(BF16)
    bank_free = {b: None for b in range(8)}
    ln = LN(r, tag + "_ln", 4)

    wq = r.sbuf(tag + "_wq", [128, 8 * 1536], BF16)
    wo = r.sbuf(tag + "_wo", [128, 4 * 1024], BF16)
    csk = r.sbuf(tag + "_csk", [128, NT * 32], F32)
    snk = r.sbuf(tag + "_snk", [128, NT * 32], F32)
    csq = r.sbuf(tag + "_csq", [128, NQ * 32], F32)
    snq = r.sbuf(tag + "_snq", [128, NQ * 32], F32)
    mt = r.sbuf(tag + "_mt", [128, 3 * 384], F32)
    bor = r.sbuf(tag + "_bor", [1, 1024], BF16)
    onesb1 = r.sbuf(tag + "_onb1", [1, 128], BF16)
    e_cs = None
    for nm, dst, srcd in [("csk", csk, csk_d), ("snk", snk, snk_d),
                          ("csq", csq, csq_d), ("snq", snq, snq_d)]:
        e_cs = r.dma("sync", out=dst[:, :].rearrange("p (t f) -> p t f", f=32),
                     in_=srcd.rearrange("(t p) f -> p t f", p=128), inc="dS")
    e_mt = r.dma("sync", out=mt[:, :].rearrange("p (m f) -> p m f", f=384),
                 in_=masks_d.rearrange("m p f -> p m f"), inc="dS")
    e_bo = r.dma("gpsimd", out=bor[:, :], in_=bout_d, inc="dG")
    eS = e_mt
    ev_onb1 = r.op("gpsimd", lambda e: e.memset(onesb1[:, :], 1.0),
                   inc="g_const")

    xnt = [r.sbuf(tag + f"_xn{p}", [128, 1024], BF16) for p in range(2)]
    xnf = [r.sbuf(tag + f"_xf{p}", [128, 1024], BF16) for p in range(2)]
    kraw = [r.sbuf(tag + f"_kr{p}", [128, 512], BF16) for p in range(2)]
    qraw = [r.sbuf(tag + f"_qr{p}", [128, 512], BF16) for p in range(2)]
    krp = [r.sbuf(tag + f"_krp{p}", [128, 512], BF16) for p in range(2)]
    rtmp = r.sbuf(tag + "_rt", [128, 512], BF16)
    rtm2 = r.sbuf(tag + "_rt2", [128, 512], BF16)
    kfm = r.sbuf(tag + "_kfm", [128, 4 * 2304], BF16)
    vtm = r.sbuf(tag + "_vtm", [128, NT * 512], BF16)
    qtm = r.sbuf(tag + "_qtm", [128, NQ * 512], BF16)
    qfm = [r.sbuf(tag + f"_qfm{p}", [128, 512], BF16) for p in range(2)]
    smt = [r.sbuf(tag + f"_sm{p}", [128, 384], F32) for p in range(2)]
    pt_s = [r.sbuf(tag + f"_pt{p}", [128, 384], BF16) for p in range(2)]
    ptt = [r.sbuf(tag + f"_ptt{p}", [128, 384], BF16) for p in range(2)]
    att = [r.sbuf(tag + f"_at{p}", [128, 512], BF16) for p in range(2)]
    atf = [r.sbuf(tag + f"_af{p}", [128, 512], BF16) for p in range(2)]
    stats = r.sbuf(tag + "_st", [128, 16], F32)
    y0d = r.nc.dram_tensor(tag + "_y0d", [128, NQ * 1024], BF16).ap()
    y0st = [r.sbuf(tag + f"_y0s{p}", [128, 1024], BF16) for p in range(2)]
    sem_ys = [r.sem(tag + "_ys0"), r.sem(tag + "_ys1")]
    sem_yl = [r.sem(tag + f"_yl{p}") for p in range(2)]
    sem_wq = r.sem(tag + "_swq")
    sem_wo = r.sem(tag + "_swo")

    last_res = None
    ystore = [None, None]
    free = {}
    wq_ev = wo_ev = None
    for half in range(2):
        wq_ev_new = []
        for gi, c0 in enumerate([half * 512, 1024 + half * 512,
                                 2048 + half * 512]):
            ev = r.dma("gpsimd",
                       out=wq[:, :].rearrange("p (c g f) -> p c g f",
                                              c=8, g=3)[:, :, gi],
                       in_=wqkvT_d[:, c0:c0 + 512].rearrange(
                           "(c p) f -> p c f", p=128),
                       waits=[free.get("wq")], inc=sem_wq)
        wq_ev = ev
        wo_ev = r.dma("gpsimd",
                      out=wo[:, :].rearrange("p (c f) -> p c f", f=1024),
                      in_=woT_d[half * 512:(half + 1) * 512, :].rearrange(
                          "(c p) f -> p c f", p=128),
                      waits=[free.get("wo")], inc=sem_wo)

        kv_done = []
        for it in range(NT):
            sl = it % 2
            lev = ln.tile(it % 4, xts(it), xnt[sl][:, :],
                          x_ready=[x_ready, free.get(f"xnt{sl}")])
            trs = []
            for c in range(8):
                b = 6 + (c % 2)
                tr = r.op("tensor", lambda e, b=b, c=c, sl=sl: e.transpose(
                    ptr[:, b * 1024:b * 1024 + 128],
                    xnt[sl][:, c * 128:(c + 1) * 128], idn[:, :]),
                    waits=[lev, idn_ev, bank_free[b], free.get(f"xnf{sl}")],
                    inc="t")
                ev = r.op("scalar", lambda e, b=b, c=c, sl=sl: e.copy(
                    xnf[sl][:, c * 128:(c + 1) * 128],
                    ptr[:, b * 1024:b * 1024 + 128]), waits=[tr], inc="a")
                bank_free[b] = ev
                trs.append(ev)
            free[f"xnt{sl}"] = tr
            xnf_ready = trs[-1]
            kb = it % 2
            mm = None
            for c in range(8):
                w0 = [bank_free[kb], xnf_ready, wq_ev] if c == 0 else []
                mm = r.op("tensor", lambda e, kb=kb, c=c, sl=sl: e.matmul(
                    pbank(kb), xnf[sl][:, c * 128:(c + 1) * 128],
                    wq[:, :].rearrange("p (c g f) -> p c g f", c=8, g=3)
                    [:, c, 1], start=(c == 0), stop=(c == 7)),
                    waits=w0, inc="t" if c == 7 else None)
            ek = r.op("vector", lambda e, kb=kb, sl=sl: e.tensor_copy(
                kraw[sl][:, :], pbank(kb)),
                waits=[mm, free.get(f"kraw{sl}")], inc="v")
            bank_free[kb] = ek
            erp = emit_rope_neox(r, krp[sl][:, :], kraw[sl][:, :],
                                 csk[:, it * 32:(it + 1) * 32],
                                 snk[:, it * 32:(it + 1) * 32],
                                 8, 64, rtmp[:, :],
                                 waits=[ek, eS, free.get(f"krp{sl}")])
            free[f"kraw{sl}"] = erp
            ktr = []
            for j in range(4):
                b = 6 + (j % 2)
                tr = r.op("tensor", lambda e, b=b, j=j, sl=sl: e.transpose(
                    ptr[:, b * 1024:b * 1024 + 128],
                    krp[sl][:, j * 128:(j + 1) * 128], idn[:, :]),
                    waits=[erp, bank_free[b]], inc="t")
                ev = r.op("scalar", lambda e, b=b, j=j, it=it: e.copy(
                    kfm[:, j * 2304 + it * 128:j * 2304 + (it + 1) * 128],
                    ptr[:, b * 1024:b * 1024 + 128]),
                    waits=[tr, free.get("kfm")], inc="a")
                bank_free[b] = ev
                ktr.append((tr, ev))
            free[f"krp{sl}"] = ktr[-1][0]
            kfm_ready = ktr[-1][1]
            vb = 2 + it % 2
            mm = None
            for c in range(8):
                w0 = [bank_free[vb], xnf_ready] if c == 0 else []
                mm = r.op("tensor", lambda e, vb=vb, c=c, sl=sl: e.matmul(
                    pbank(vb), xnf[sl][:, c * 128:(c + 1) * 128],
                    wq[:, :].rearrange("p (c g f) -> p c g f", c=8, g=3)
                    [:, c, 2], start=(c == 0), stop=(c == 7)),
                    waits=w0, inc="t" if c == 7 else None)
            evv = r.op("vector", lambda e, vb=vb, it=it: e.tensor_copy(
                vtm[:, it * 512:(it + 1) * 512], pbank(vb)),
                waits=[mm, free.get("vtm")], inc="v")
            bank_free[vb] = evv
            kv_done.append((kfm_ready, evv))
            if 1 <= it <= NQ:
                qi = it - 1
                qb = 4 + it % 2
                mm = None
                for c in range(8):
                    w0 = [bank_free[qb], xnf_ready] if c == 0 else []
                    mm = r.op("tensor", lambda e, qb=qb, c=c, sl=sl: e.matmul(
                        pbank(qb), xnf[sl][:, c * 128:(c + 1) * 128],
                        wq[:, :].rearrange("p (c g f) -> p c g f", c=8, g=3)
                        [:, c, 0], start=(c == 0), stop=(c == 7)),
                        waits=w0, inc="t" if c == 7 else None)
                eq = r.op("vector", lambda e, qb=qb, sl=sl: e.tensor_copy(
                    qraw[sl][:, :], pbank(qb)),
                    waits=[mm, free.get(f"qraw{sl}")], inc="v")
                bank_free[qb] = eq
                eqr = emit_rope_neox(
                    r, qtm[:, qi * 512:(qi + 1) * 512], qraw[sl][:, :],
                    csq[:, qi * 32:(qi + 1) * 32],
                    snq[:, qi * 32:(qi + 1) * 32], 8, 64, rtm2[:, :],
                    waits=[eq, eS, free.get("qtm")])
                free[f"qraw{sl}"] = eqr
                qtm_ready = eqr
            free[f"xnf{sl}"] = mm

        for qi in range(NQ):
            sl = qi % 2
            if half == 1:
                yl = r.dma("scalar", out=y0st[sl][:, :],
                           in_=y0d[:, qi * 1024:(qi + 1) * 1024],
                           waits=[ystore[0], ystore[1],
                                  free.get(f"y0st{sl}")],
                           inc=sem_yl[sl])
            qtr = []
            for j in range(4):
                b = 6 + (j % 2)
                tr = r.op("tensor", lambda e, b=b, j=j, qi=qi: e.transpose(
                    ptr[:, b * 1024:b * 1024 + 128],
                    qtm[:, qi * 512 + j * 128:qi * 512 + (j + 1) * 128],
                    idn[:, :]), waits=[qtm_ready, bank_free[b],
                                       free.get(f"qfm{sl}")], inc="t")
                ev = r.op("scalar", lambda e, b=b, j=j, sl=sl: e.copy(
                    qfm[sl][:, j * 128:(j + 1) * 128],
                    ptr[:, b * 1024:b * 1024 + 128]), waits=[tr], inc="a")
                bank_free[b] = ev
                qtr.append(ev)
            qfm_ready = qtr[-1]
            mi = 0 if qi == 0 else (2 if qi == NQ - 1 else 1)
            aev = None
            for h in range(8):
                hs = (h % 2) * 64
                j = h // 2
                sb = h % 2
                mm = r.op("tensor", lambda e, sb=sb, j=j, hs=hs, sl=sl, qi=qi:
                          e.matmul(pbank(sb, 384),
                                   qfm[sl][:, j * 128:(j + 1) * 128]
                                   [hs:hs + 64, :],
                                   kfm[:, j * 2304 + qi * 128:
                                       j * 2304 + qi * 128 + 384]
                                   [hs:hs + 64, :],
                                   start=True, stop=True),
                          waits=[qfm_ready, kv_done[-1][0], bank_free[sb]],
                          inc="t")
                e1 = r.op("vector", lambda e, sb=sb, mi=mi, sl=sl: e.tensor_add(
                    smt[sl][:, :], pbank(sb, 384),
                    mt[:, mi * 384:(mi + 1) * 384]),
                    waits=[mm, e_mt, free.get(f"smt{sl}")], inc="v")
                bank_free[sb] = e1
                e2 = r.op("vector", lambda e, sl=sl: e.reduce_max(
                    stats[:, 0:1], smt[sl][:, :], axis=AX.X, negate=True),
                    waits=[e1], inc="v")
                e3 = r.op("scalar", lambda e, sl=sl: e.activation(
                    out=pt_s[sl][:, :], in_=smt[sl][:, :], func=AF.Exp,
                    bias=stats[:, 0:1], scale=1.0, accum_out=stats[:, 1:2]),
                    waits=[e2, free.get(f"pt{sl}")], inc="a")
                free[f"smt{sl}"] = e3
                e4 = r.op("vector", lambda e: e.reciprocal(
                    stats[:, 2:3], stats[:, 1:2]), waits=[e3], inc="v")
                pvb = 2 + h % 2
                mmpv = None
                for cc in range(3):
                    b = 6 + (cc % 2)
                    tr = r.op("tensor", lambda e, b=b, cc=cc, sl=sl: e.transpose(
                        ptr[:, b * 1024:b * 1024 + 128],
                        pt_s[sl][:, cc * 128:(cc + 1) * 128], idn[:, :]),
                        waits=[e3, bank_free[b], free.get(f"ptt{sl}")], inc="t")
                    ec = r.op("scalar", lambda e, b=b, cc=cc, sl=sl: e.copy(
                        ptt[sl][:, cc * 128:(cc + 1) * 128],
                        ptr[:, b * 1024:b * 1024 + 128]), waits=[tr], inc="a")
                    bank_free[b] = ec
                    w0 = [ec]
                    if cc == 0:
                        w0 += [bank_free[pvb], kv_done[-1][1]]
                    mmpv = r.op("tensor", lambda e, pvb=pvb, cc=cc, sl=sl,
                                qi=qi, h=h, hs=hs:
                                e.matmul(pbank(pvb, 64),
                                         ptt[sl][:, cc * 128:(cc + 1) * 128],
                                         vtm[:, (qi + cc) * 512 + h * 64:
                                             (qi + cc) * 512 + (h + 1) * 64],
                                         start=(cc == 0), stop=(cc == 2)),
                                waits=w0, inc="t" if cc == 2 else None)
                free[f"ptt{sl}"] = mmpv
                free[f"pt{sl}"] = mmpv
                aev = r.op("vector", lambda e, pvb=pvb, h=h, sl=sl:
                           e.tensor_scalar(
                               out=att[sl][:, h * 64:(h + 1) * 64],
                               in0=pbank(pvb, 64), scalar1=stats[:, 2:3],
                               scalar2=None, op0=ALU.mult),
                           waits=[mmpv, e4, free.get(f"att{sl}")], inc="v")
                bank_free[pvb] = aev
            free[f"qfm{sl}"] = mm
            atr = []
            for j in range(4):
                b = 6 + (j % 2)
                tr = r.op("tensor", lambda e, b=b, j=j, sl=sl: e.transpose(
                    ptr[:, b * 1024:b * 1024 + 128],
                    att[sl][:, j * 128:(j + 1) * 128], idn[:, :]),
                    waits=[aev, bank_free[b], free.get(f"atf{sl}")], inc="t")
                ec = r.op("scalar", lambda e, b=b, j=j, sl=sl: e.copy(
                    atf[sl][:, j * 128:(j + 1) * 128],
                    ptr[:, b * 1024:b * 1024 + 128]), waits=[tr], inc="a")
                bank_free[b] = ec
                atr.append(ec)
            free[f"att{sl}"] = tr
            atf_ready = atr[-1]
            for eh in range(2):
                yb = 4 + eh
                mm = None
                for c in range(4):
                    w0 = [atf_ready, wo_ev, bank_free[yb]] if c == 0 else []
                    mm = r.op("tensor", lambda e, yb=yb, c=c, eh=eh, sl=sl,
                              half=half:
                              e.matmul(pbank(yb),
                                       atf[sl][:, c * 128:(c + 1) * 128],
                                       wo[:, :].rearrange(
                                           "p (c f) -> p c f", f=1024)
                                       [:, c, eh * 512:(eh + 1) * 512],
                                       start=(c == 0),
                                       stop=(c == 3 and half == 1)),
                              waits=w0, inc=None if (c == 3 and half == 0)
                              else ("t" if c == 3 else None))
                if half == 0:
                    mm = r.op("tensor", lambda e, yb=yb, eh=eh: e.matmul(
                        pbank(yb), onesb1[:, :],
                        bor[:, eh * 512:(eh + 1) * 512],
                        start=False, stop=True),
                        waits=[e_bo, ev_onb1], inc="t")
                xsl = xts(qi + 1)[:, eh * 512:(eh + 1) * 512]
                ysl = y0st[sl][:, eh * 512:(eh + 1) * 512]
                if half == 0:
                    ry = r.op("vector", lambda e, ysl=ysl, yb=yb:
                              e.tensor_copy(ysl, pbank(yb)),
                              waits=[mm, free.get(f"y0st{sl}")],
                              inc="v")
                    last_res = ry
                else:
                    ry = r.op("vector", lambda e, yb=yb, xsl=xsl:
                              e.tensor_add(xsl, xsl, pbank(yb)),
                              waits=[mm, yl], inc="v")
                    ry2 = r.op("vector", lambda e, xsl=xsl, ysl=ysl:
                               e.tensor_add(xsl, xsl, ysl),
                               waits=[ry], inc="v")
                    last_res = ry2
                bank_free[yb] = last_res
            if half == 0:
                ys = r.dma("scalar", out=y0d[:, qi * 1024:(qi + 1) * 1024],
                           in_=y0st[sl][:, :], waits=[last_res], inc=sem_ys[sl])
                free[f"y0st{sl}"] = ys
                ystore[sl] = ys
            else:
                free[f"y0st{sl}"] = last_res
            free[f"atf{sl}"] = mm
        free["wq"] = mm
        free["wo"] = mm
        free["vtm"] = mmpv
        free["kfm"] = mmpv
        free["qtm"] = mmpv
    return last_res


def emit_rope_gpt(r, dst, src, cos_ap, sin_ap, width, tmp, waits=(), extra=()):
    hw = width // 2
    ev_ = lambda a, o: a.rearrange("p (f two) -> p f two", two=2)[:, :, o]
    xe = ev_(src, 0)
    xo = ev_(src, 1)
    t1 = tmp[:, 0:hw]
    t2 = tmp[:, hw:2 * hw]
    e1 = r.op("vector", lambda e: e.tensor_mul(t1, xe, cos_ap),
              waits=tuple(waits) + tuple(extra), inc="v")
    e2 = r.op("vector", lambda e: e.tensor_mul(t2, xo, sin_ap), inc="v")
    e3 = r.op("vector", lambda e: e.tensor_sub(ev_(dst, 0), t1, t2),
              waits=[e1, e2], inc="v")
    e4 = r.op("vector", lambda e: e.tensor_mul(t1, xe, sin_ap), waits=[e3],
              inc="v")
    e5 = r.op("vector", lambda e: e.tensor_mul(t2, xo, cos_ap), inc="v")
    e6 = r.op("vector", lambda e: e.tensor_add(ev_(dst, 1), t1, t2),
              waits=[e4, e5], inc="v")
    return e6



def emit_s5(r, xt, x_ready, xrot_d, wq_d, wk_d, wv_d, wo_d, cq_d, sq_d,
            ck_d, sk_d, tag="s5", idn=None, idn_ev=None, fused=None):
    NT = 64
    NQ = 16
    NSEG = 16

    if idn is None:
        idn, idn_ev = make_id128(r, BF16, tag + "_id")
    pb = r.psum(tag + "_pb", [128, 4096], F32)
    pbank = lambda b, w=512: pb[:, b * 512:b * 512 + w]
    ptr = pb.bitcast(BF16)
    bank_free = {b: None for b in range(8)}
    ln = LN(r, tag + "_ln", 4)

    kfm_d = r.nc.dram_tensor(tag + "_kfm", [64, 128, 8, 128], BF16).ap()
    vtm_d = r.nc.dram_tensor(tag + "_vtm", [8192, 1024], BF16).ap()

    wq = r.sbuf(tag + "_wq", [128, 8 * 1024], BF16)
    wk = r.sbuf(tag + "_wk", [128, 8 * 512], BF16)
    wv = r.sbuf(tag + "_wv", [128, 8 * 512], BF16)
    wo = wq
    e_w = None
    for dst, srcd in [(wq, wq_d), (wk, wk_d), (wv, wv_d)]:
        e_w = r.dma("gpsimd", out=dst[:, 0:srcd.shape[1] * 8].rearrange(
            "p (c f) -> p c f", c=8),
            in_=srcd.rearrange("(c p) f -> p c f", p=128), inc="dG")
    cks = [r.sbuf(tag + f"_ck{p}", [128, 256], F32) for p in range(2)]
    sks = [r.sbuf(tag + f"_sk{p}", [128, 256], F32) for p in range(2)]
    sem_tq = [r.sem(tag + f"_tq{p}") for p in range(2)]
    sem_tk = [r.sem(tag + f"_tk{p}") for p in range(2)]
    xrt = [r.sbuf(tag + f"_xr{p}", [128, 1024], F32) for p in range(2)]
    sem_xr = [r.sem(tag + f"_xr{p}") for p in range(2)]
    xnt = [r.sbuf(tag + f"_xn{p}", [128, 1024], BF16) for p in range(2)]
    xnf = [r.sbuf(tag + f"_xf{p}", [128, 1024], BF16) for p in range(2)]
    kraw = [r.sbuf(tag + f"_kr{p}", [128, 512], BF16) for p in range(2)]
    krp = [r.sbuf(tag + f"_kp{p}", [128, 512], BF16) for p in range(2)]
    arena = r.sbuf(tag + "_arena", [128, 8192], BF16)
    krep = [arena[:, 0:1024], arena[:, 1024:2048]]
    vrep = [arena[:, 2048:3072], arena[:, 3072:4096]]
    kslab = [arena[:, 4096:5120], arena[:, 5120:6144]]
    qraw = [arena[:, 6144:7168], arena[:, 7168:8192]]
    rtmp = r.sbuf(tag + "_rt", [128, 1024], BF16)
    qtm_d = r.nc.dram_tensor(tag + "_qtm", [128, NQ * 1024], BF16).ap()
    qtm = [r.sbuf(tag + f"_qt{p}", [128, 1024], BF16) for p in range(2)]
    qld = [r.sbuf(tag + f"_ql{p}", [128, 1024], BF16) for p in range(2)]
    sem_qs = [r.sem(tag + f"_qs{p}") for p in range(2)]
    sem_ql = [r.sem(tag + f"_ql{p}") for p in range(2)]
    sem_ks = [r.sem(tag + f"_ks{p}") for p in range(2)]
    sem_vs = [r.sem(tag + f"_vs{p}") for p in range(2)]
    qfm = [r.sbuf(tag + f"_qf{p}", [128, 1024], BF16) for p in range(2)]
    kseg = [r.sbuf(tag + f"_kg{p}", [128, 8 * 512], BF16) for p in range(2)]
    vseg = [r.sbuf(tag + f"_vg{p}", [128, 4 * 1024], BF16) for p in range(2)]
    _ksf = kseg[0].bitcast(F32)
    cqs = [_ksf[:, 0:512], _ksf[:, 1024:1536]]
    sqs = [_ksf[:, 512:1024], _ksf[:, 1536:2048]]
    sem_kg = [r.sem(tag + f"_kg{p}") for p in range(2)]
    sem_vg = [r.sem(tag + f"_vg{p}") for p in range(2)]
    pexp = [arena[:, 0:2048], arena[:, 2048:4096]]
    ptts = [arena[:, 4096:6144], arena[:, 6144:8192]]
    oseg = r.sbuf(tag + "_os", [128, NSEG * 256], BF16)
    msegs = r.sbuf(tag + "_ms", [128, NSEG * 4], F32)
    zsegs = r.sbuf(tag + "_zs", [128, NSEG * 4], F32)
    alph = r.sbuf(tag + "_al", [128, NSEG * 4], F32)
    azs = r.sbuf(tag + "_az", [128, NSEG * 4], F32)
    stats = r.sbuf(tag + "_sst", [128, 8], F32)
    ofin = r.sbuf(tag + "_of", [128, 4 * 64], F32)
    attn = r.sbuf(tag + "_attn", [128, 1024], BF16)
    atf = [r.sbuf(tag + f"_atf{p}", [128, 1024], BF16) for p in range(2)]

    free = {}
    for it in range(NT):
        sl = it % 2
        if fused is None:
            exr = r.dma("sync", out=xrt[sl][:, :],
                        in_=xrot_d[it * 128:(it + 1) * 128, :],
                        waits=[free.get(f"xrt{sl}")], inc=sem_xr[sl])
            lev = ln.tile(it % 4, xrt[sl][:, :], xnt[sl][:, :],
                          x_ready=[exr, free.get(f"xnt{sl}")])
            free[f"xrt{sl}"] = lev
        else:
            lev = r.dma("sync", out=xnt[sl][:, :],
                        in_=fused["xnf_d"][it * 128:(it + 1) * 128, :],
                        waits=[free.get(f"xnt{sl}"), fused["ag_ev"]],
                        inc=sem_xr[sl])
        trs = None
        for c in range(8):
            b = 6 + (c % 2)
            tr = r.op("tensor", lambda e, b=b, c=c, sl=sl: e.transpose(
                ptr[:, b * 1024:b * 1024 + 128],
                xnt[sl][:, c * 128:(c + 1) * 128], idn[:, :]),
                waits=[lev, idn_ev, bank_free[b], free.get(f"xnf{sl}")],
                inc="t")
            ev = r.op("scalar", lambda e, b=b, c=c, sl=sl: e.copy(
                xnf[sl][:, c * 128:(c + 1) * 128],
                ptr[:, b * 1024:b * 1024 + 128]), waits=[tr], inc="a")
            bank_free[b] = ev
            trs = ev
            free[f"xnt{sl}"] = tr
        kb = it % 2
        mm = None
        for c in range(8):
            w0 = [bank_free[kb], trs, e_w] if c == 0 else []
            mm = r.op("tensor", lambda e, kb=kb, c=c, sl=sl: e.matmul(
                pbank(kb), xnf[sl][:, c * 128:(c + 1) * 128],
                wk[:, c * 512:(c + 1) * 512], start=(c == 0), stop=(c == 7)),
                waits=w0, inc="t" if c == 7 else None)
        ek = r.op("vector", lambda e, kb=kb, sl=sl: e.tensor_copy(
            kraw[sl][:, :], pbank(kb)), waits=[mm, free.get(f"kraw{sl}")],
            inc="v")
        bank_free[kb] = ek
        etk = r.dma("sync", out=cks[sl][:, :],
                    in_=ck_d[it * 128:(it + 1) * 128, :],
                    waits=[free.get(f"cks{sl}")], inc=sem_tk[sl])
        etk2 = r.dma("sync", out=sks[sl][:, :],
                     in_=sk_d[it * 128:(it + 1) * 128, :],
                     waits=[free.get(f"cks{sl}")], inc=sem_tk[sl])
        erp = emit_rope_gpt(r, krp[sl][:, :], kraw[sl][:, :], cks[sl][:, :],
                            sks[sl][:, :], 512, rtmp[:, 0:512],
                            waits=[ek, etk2], extra=[free.get(f"krp{sl}")])
        free[f"kraw{sl}"] = erp
        free[f"cks{sl}"] = erp
        er1 = r.op("vector", lambda e, sl=sl: e.tensor_copy(
            krep[sl][:, :].rearrange("p (f two) -> p f two", two=2)[:, :, 0],
            krp[sl][:, :]), waits=[erp, free.get(f"krep{sl}")], inc="v")
        er2 = r.op("vector", lambda e, sl=sl: e.tensor_copy(
            krep[sl][:, :].rearrange("p (f two) -> p f two", two=2)[:, :, 1],
            krp[sl][:, :]), waits=[er1], inc="v")
        free[f"krp{sl}"] = er2
        ks_ev = None
        for j in range(8):
            b = 6 + (j % 2)
            tr = r.op("tensor", lambda e, b=b, j=j, sl=sl: e.transpose(
                ptr[:, b * 1024:b * 1024 + 128],
                krep[sl][:, j * 128:(j + 1) * 128], idn[:, :]),
                waits=[er2, bank_free[b], free.get(f"kslab{sl}")], inc="t")
            ev = r.op("scalar", lambda e, b=b, j=j, sl=sl: e.copy(
                kslab[sl][:, j * 128:(j + 1) * 128],
                ptr[:, b * 1024:b * 1024 + 128]), waits=[tr], inc="a")
            bank_free[b] = ev
            ks_ev = ev
        free[f"krep{sl}"] = tr
        eks = r.dma("sync", out=kfm_d[it].rearrange("p c j -> p (c j)"),
                    in_=kslab[sl][:, :], waits=[ks_ev], inc=sem_ks[sl])
        free[f"kslab{sl}"] = eks
        vb = 2 + it % 2
        mm = None
        for c in range(8):
            w0 = [bank_free[vb], trs] if c == 0 else []
            mm = r.op("tensor", lambda e, vb=vb, c=c, sl=sl: e.matmul(
                pbank(vb), xnf[sl][:, c * 128:(c + 1) * 128],
                wv[:, c * 512:(c + 1) * 512], start=(c == 0), stop=(c == 7)),
                waits=w0, inc="t" if c == 7 else None)
        ev1 = r.op("vector", lambda e, vb=vb, sl=sl: e.tensor_copy(
            vrep[sl][:, :].rearrange("p (f two) -> p f two", two=2)[:, :, 0],
            pbank(vb)), waits=[mm, free.get(f"vrep{sl}")], inc="v")
        ev2 = r.op("vector", lambda e, vb=vb, sl=sl: e.tensor_copy(
            vrep[sl][:, :].rearrange("p (f two) -> p f two", two=2)[:, :, 1],
            pbank(vb)), waits=[ev1], inc="v")
        bank_free[vb] = ev2
        evs = r.dma("sync", out=vtm_d[it * 128:(it + 1) * 128, :],
                    in_=vrep[sl][:, :], waits=[ev2], inc=sem_vs[sl])
        free[f"vrep{sl}"] = evs
        if fused is None and it < NQ:
            qb = 4 + it % 2
            mm = None
            for hf in range(2):
                for c in range(8):
                    w0 = [bank_free[qb], trs] if c == 0 else []
                    mm = r.op("tensor", lambda e, qb=qb, c=c, hf=hf, sl=sl:
                              e.matmul(
                                  pbank(qb),
                                  xnf[sl][:, c * 128:(c + 1) * 128],
                                  wq[:, c * 1024 + hf * 512:
                                     c * 1024 + (hf + 1) * 512],
                                  start=(c == 0), stop=(c == 7)),
                              waits=w0, inc="t" if c == 7 else None)
                eq = r.op("vector", lambda e, qb=qb, sl=sl, hf=hf: e.tensor_copy(
                    qraw[sl][:, hf * 512:(hf + 1) * 512], pbank(qb)),
                    waits=[mm, free.get(f"qraw{sl}")] if hf == 0 else [mm],
                    inc="v")
                bank_free[qb] = eq
            etq = r.dma("sync", out=cqs[sl][:, :],
                        in_=cq_d[it * 128:(it + 1) * 128, :],
                        waits=[free.get(f"cqs{sl}")], inc=sem_tq[sl])
            etq2 = r.dma("sync", out=sqs[sl][:, :],
                         in_=sq_d[it * 128:(it + 1) * 128, :],
                         waits=[free.get(f"cqs{sl}")], inc=sem_tq[sl])
            eqr = emit_rope_gpt(r, qtm[sl][:, :],
                                qraw[sl][:, :], cqs[sl][:, :], sqs[sl][:, :],
                                1024, rtmp[:, :],
                                waits=[eq, etq2], extra=[free.get(f"qtm{sl}")])
            free[f"qraw{sl}"] = eqr
            free[f"cqs{sl}"] = eqr
            eqs_ = r.dma("scalar", out=qtm_d[:, it * 1024:(it + 1) * 1024],
                         in_=qtm[sl][:, :], waits=[eqr], inc=sem_qs[sl])
            free[f"qtm{sl}"] = eqs_
            qstore = eqs_
    if fused is not None:
        for it in range(NQ):
            sl = it % 2
            lev = r.dma("sync", out=xnt[sl][:, :],
                        in_=fused["xnq_d"][it * 128:(it + 1) * 128, :],
                        waits=[free.get(f"xnt{sl}"), fused["lnq_ev"],
                               fused.get("lnq_ev0")],
                        inc=sem_xr[sl])
            trs = None
            for c in range(8):
                b = 6 + (c % 2)
                tr = r.op("tensor", lambda e, b=b, c=c, sl=sl: e.transpose(
                    ptr[:, b * 1024:b * 1024 + 128],
                    xnt[sl][:, c * 128:(c + 1) * 128], idn[:, :]),
                    waits=[lev, idn_ev, bank_free[b], free.get(f"xnf{sl}")],
                    inc="t")
                ev = r.op("scalar", lambda e, b=b, c=c, sl=sl: e.copy(
                    xnf[sl][:, c * 128:(c + 1) * 128],
                    ptr[:, b * 1024:b * 1024 + 128]), waits=[tr], inc="a")
                bank_free[b] = ev
                trs = ev
                free[f"xnt{sl}"] = tr
            qb = 4 + it % 2
            mm = None
            for hf in range(2):
                for c in range(8):
                    w0 = [bank_free[qb], trs] if c == 0 else []
                    mm = r.op("tensor", lambda e, qb=qb, c=c, hf=hf, sl=sl:
                              e.matmul(
                                  pbank(qb),
                                  xnf[sl][:, c * 128:(c + 1) * 128],
                                  wq[:, c * 1024 + hf * 512:
                                     c * 1024 + (hf + 1) * 512],
                                  start=(c == 0), stop=(c == 7)),
                              waits=w0, inc="t" if c == 7 else None)
                eq = r.op("vector", lambda e, qb=qb, sl=sl, hf=hf:
                          e.tensor_copy(
                              qraw[sl][:, hf * 512:(hf + 1) * 512], pbank(qb)),
                          waits=[mm, free.get(f"qraw{sl}")] if hf == 0
                          else [mm], inc="v")
                bank_free[qb] = eq
            free[f"xnf{sl}"] = mm
            etq = r.dma("sync", out=cqs[sl][:, :],
                        in_=cq_d[it * 128:(it + 1) * 128, :],
                        waits=[free.get(f"cqs{sl}")], inc=sem_tq[sl])
            etq2 = r.dma("sync", out=sqs[sl][:, :],
                         in_=sq_d[it * 128:(it + 1) * 128, :],
                         waits=[free.get(f"cqs{sl}")], inc=sem_tq[sl])
            eqr = emit_rope_gpt(r, qtm[sl][:, :],
                                qraw[sl][:, :], cqs[sl][:, :], sqs[sl][:, :],
                                1024, rtmp[:, :],
                                waits=[eq, etq2], extra=[free.get(f"qtm{sl}")])
            free[f"qraw{sl}"] = eqr
            free[f"cqs{sl}"] = eqr
            eqs_ = r.dma("scalar", out=qtm_d[:, it * 1024:(it + 1) * 1024],
                         in_=qtm[sl][:, :], waits=[eqr], inc=sem_qs[sl])
            free[f"qtm{sl}"] = eqs_
            qstore = eqs_
    phaseA_pe = mm
    e_wo = r.dma("gpsimd", out=wo[:, :].rearrange("p (c f) -> p c f", c=8),
                 in_=wo_d.rearrange("(c p) f -> p c f", p=128),
                 waits=[phaseA_pe], inc="dG")

    free["kseg0"] = free.get("cqs1") or free.get("cqs0")
    free["pexp0"] = free.get("kslab0") or qstore
    free["pexp1"] = free.get("kslab1") or qstore
    arena_guards = [free.get("krep0"), free.get("krep1"),
                    free.get("vrep0"), free.get("vrep1"),
                    free.get("qtm0"), free.get("qtm1"), qstore]
    free["ptts0"] = free.get("kslab0")
    free["ptts1"] = free.get("kslab1")
    kstot = [Event(sem_ks[0], sem_ks[0].n), Event(sem_ks[1], sem_ks[1].n)]
    vstot = [Event(sem_vs[0], sem_vs[0].n), Event(sem_vs[1], sem_vs[1].n)]
    last_res = None
    for qi in range(NQ):
        sl = qi % 2
        eql = r.dma("scalar", out=qld[sl][:, :],
                    in_=qtm_d[:, qi * 1024:(qi + 1) * 1024],
                    waits=[Event(sem_qs[0], sem_qs[0].n),
                           Event(sem_qs[1], sem_qs[1].n),
                           free.get(f"qld{sl}")], inc=sem_ql[sl])
        qtr = None
        for j in range(8):
            b = 6 + (j % 2)
            tr = r.op("tensor", lambda e, b=b, j=j, qi=qi, sl=sl: e.transpose(
                ptr[:, b * 1024:b * 1024 + 128],
                qld[sl][:, j * 128:(j + 1) * 128],
                idn[:, :]), waits=[eql, bank_free[b],
                                   free.get(f"qfm{sl}")], inc="t")
            ev = r.op("scalar", lambda e, b=b, j=j, sl=sl: e.copy(
                qfm[sl][:, j * 128:(j + 1) * 128],
                ptr[:, b * 1024:b * 1024 + 128]), waits=[tr], inc="a")
            bank_free[b] = ev
            qtr = ev
        for hg in range(4):
            oe = None
            for sg in range(NSEG):
                sgl = sg % 2
                ekg = r.dma("gpsimd", out=kseg[sgl][:, :].rearrange(
                    "p (c t j) -> p c t j", c=8, t=4),
                    in_=kfm_d[sg * 4:(sg + 1) * 4].rearrange(
                        "t p c j -> p c t j"),
                    waits=[free.get(f"kseg{sgl}"), kstot[0], kstot[1]],
                    inc=sem_kg[sgl])
                evg = r.dma("gpsimd", out=vseg[sgl][:, :].rearrange(
                    "p (t f) -> p t f", t=4),
                    in_=vtm_d[sg * 512:(sg + 1) * 512, :].rearrange(
                        "(t p) f -> p t f", p=128),
                    waits=[free.get(f"vseg{sgl}"), vstot[0], vstot[1]],
                    inc=sem_vg[sgl])
                mms = []
                for r4 in range(4):
                    h = hg * 4 + r4
                    hj = h // 2
                    hb = (h % 2) * 64
                    mm = r.op("tensor", lambda e, r4=r4, hj=hj, hb=hb, sl=sl,
                              sgl=sgl: e.matmul(
                                  pbank(r4),
                                  qfm[sl][:, hj * 128:(hj + 1) * 128]
                                  [hb:hb + 64, :],
                                  kseg[sgl][:, hj * 512:(hj + 1) * 512]
                                  [hb:hb + 64, :],
                                  start=True, stop=True,
                                  tile_position=(hb + (hj % 2) * 0, 0)
                                  if False else None),
                              waits=[qtr, ekg, bank_free[r4]], inc="t")
                    mms.append(mm)
                e1 = r.op("vector", lambda e, sg=sg: e.reduce_max(
                    msegs[:, sg * 4:sg * 4 + 4],
                    pb[:, 0:2048].rearrange("p (h k) -> p h k", h=4),
                    axis=AX.X, negate=True),
                    waits=mms, inc="v")
                e2s = []
                for r4 in range(4):
                    e2 = r.op("scalar", lambda e, r4=r4, sg=sg, sgl=sgl:
                              e.activation(
                                  out=pexp[sgl][:, r4 * 512:(r4 + 1) * 512],
                                  in_=pbank(r4), func=AF.Exp,
                                  bias=msegs[:, sg * 4 + r4:sg * 4 + r4 + 1],
                                  scale=1.0,
                                  accum_out=zsegs[:, sg * 4 + r4:
                                                  sg * 4 + r4 + 1]),
                              waits=[e1] + _fl(free.get(f"pexp{sgl}"))
                              + arena_guards, inc="a")
                    e2s.append(e2)
                    bank_free[r4] = e2
                mmpv = {}
                ltr = None
                for r4 in range(4):
                    h = hg * 4 + r4
                    for cc in range(4):
                        b = 6 + ((r4 * 4 + cc) % 2)
                        tr = r.op("tensor", lambda e, b=b, r4=r4, cc=cc,
                                  sgl=sgl: e.transpose(
                                      ptr[:, b * 1024:b * 1024 + 128],
                                      pexp[sgl][:, r4 * 512 + cc * 128:
                                                r4 * 512 + (cc + 1) * 128],
                                      idn[:, :]),
                                  waits=[e2s[r4], bank_free[b],
                                         free.get(f"ptts{sgl}")], inc="t")
                        ec = r.op("scalar", lambda e, b=b, r4=r4, cc=cc,
                                  sgl=sgl: e.copy(
                                      ptts[sgl][:, r4 * 512 + cc * 128:
                                                r4 * 512 + (cc + 1) * 128],
                                      ptr[:, b * 1024:b * 1024 + 128]),
                                  waits=[tr], inc="a")
                        bank_free[b] = ec
                        ltr = tr
                        w0 = [ec]
                        if cc == 0:
                            w0 += [e2s[r4], evg]
                        mm = r.op("tensor", lambda e, r4=r4, cc=cc, sgl=sgl,
                                  h=h: e.matmul(
                                      pbank(r4, 64),
                                      ptts[sgl][:, r4 * 512 + cc * 128:
                                                r4 * 512 + (cc + 1) * 128],
                                      vseg[sgl][:, cc * 1024 + h * 64:
                                                cc * 1024 + (h + 1) * 64],
                                      start=(cc == 0), stop=(cc == 3)),
                                  waits=w0, inc="t" if cc == 3 else None)
                        if cc == 3:
                            mmpv[r4] = mm
                free[f"ptts{sgl}"] = mm
                free[f"pexp{sgl}"] = ltr
                free[f"kseg{sgl}"] = mms[-1]
                free[f"vseg{sgl}"] = mm
                oe = None
                for r4 in range(4):
                    oe = r.op("vector", lambda e, sg=sg, r4=r4: e.tensor_copy(
                        oseg[:, sg * 256 + r4 * 64:sg * 256 + (r4 + 1) * 64],
                        pbank(r4, 64)),
                        waits=[mmpv[r4], free.get("oseg")], inc="v")
                    bank_free[r4] = oe
            hview = lambda a: a.rearrange("p (s h) -> p h s", h=4)
            c1 = r.op("vector", lambda e: e.tensor_reduce(
                stats[:, 0:4], hview(msegs[:, :]), axis=AX.X, op=ALU.min),
                waits=[oe], inc="v")
            attn_ev = None
            for r4 in range(4):
                h = hg * 4 + r4
                c2 = r.op("scalar", lambda e, r4=r4: e.activation(
                    out=alph[:, r4 * NSEG:(r4 + 1) * NSEG],
                    in_=msegs[:, :].rearrange("p (s h) -> p h s", h=4)[:, r4],
                    func=AF.Exp, bias=stats[:, r4:r4 + 1], scale=-1.0),
                    waits=[c1], inc="a")
                c3 = r.op("vector", lambda e, r4=r4: e.tensor_mul(
                    azs[:, r4 * NSEG:(r4 + 1) * NSEG],
                    alph[:, r4 * NSEG:(r4 + 1) * NSEG],
                    zsegs[:, :].rearrange("p (s h) -> p h s", h=4)[:, r4]),
                    waits=[c2], inc="v")
                c4 = r.op("vector", lambda e, r4=r4: e.reduce_sum(
                    stats[:, 4 + r4:5 + r4],
                    azs[:, r4 * NSEG:(r4 + 1) * NSEG], axis=AX.X),
                    waits=[c3], inc="v")
                co = None
                for sg2 in range(NSEG):
                    osl = oseg[:, sg2 * 256 + r4 * 64:sg2 * 256 + (r4 + 1) * 64]
                    if sg2 == 0:
                        co = r.op("vector", lambda e, osl=osl, r4=r4:
                                  e.tensor_scalar(
                                      out=ofin[:, r4 * 64:(r4 + 1) * 64],
                                      in0=osl,
                                      scalar1=alph[:, r4 * NSEG:r4 * NSEG + 1],
                                      scalar2=None, op0=ALU.mult),
                                  waits=[c2, free.get("ofin")], inc="v")
                    else:
                        co = r.op("vector", lambda e, osl=osl, r4=r4, sg2=sg2:
                                  e.scalar_tensor_tensor(
                                      out=ofin[:, r4 * 64:(r4 + 1) * 64],
                                      in0=osl,
                                      scalar=alph[:, r4 * NSEG + sg2:
                                                  r4 * NSEG + sg2 + 1],
                                      in1=ofin[:, r4 * 64:(r4 + 1) * 64],
                                      op0=ALU.mult, op1=ALU.add),
                                  waits=[co], inc="v")
                c5 = r.op("vector", lambda e, r4=r4: e.reciprocal(
                    stats[:, 4 + r4:5 + r4], stats[:, 4 + r4:5 + r4]),
                    waits=[c4], inc="v")
                ca = r.op("vector", lambda e, h=h, r4=r4: e.tensor_scalar(
                    out=attn[:, h * 64:(h + 1) * 64],
                    in0=ofin[:, r4 * 64:(r4 + 1) * 64],
                    scalar1=stats[:, 4 + r4:5 + r4], scalar2=None,
                    op0=ALU.mult),
                    waits=[co, c5, free.get("attn")], inc="v")
                attn_ev = ca
            free["ofin"] = attn_ev
            free["oseg"] = attn_ev
        atr = None
        for j in range(8):
            b = 6 + (j % 2)
            tr = r.op("tensor", lambda e, b=b, j=j, sl=sl: e.transpose(
                ptr[:, b * 1024:b * 1024 + 128],
                attn[:, j * 128:(j + 1) * 128], idn[:, :]),
                waits=[attn_ev, bank_free[b], free.get(f"atf{sl}")], inc="t")
            ec = r.op("scalar", lambda e, b=b, j=j, sl=sl: e.copy(
                atf[sl][:, j * 128:(j + 1) * 128],
                ptr[:, b * 1024:b * 1024 + 128]), waits=[tr], inc="a")
            bank_free[b] = ec
            atr = ec
        free["attn"] = tr
        for eh in range(2):
            yb = 2 + eh
            mm = None
            for c in range(8):
                w0 = [atr, bank_free[yb], e_wo] if c == 0 else []
                mm = r.op("tensor", lambda e, yb=yb, c=c, eh=eh, sl=sl:
                          e.matmul(pbank(yb), atf[sl][:, c * 128:(c + 1) * 128],
                                   wo[:, c * 1024 + eh * 512:
                                      c * 1024 + (eh + 1) * 512],
                                   start=(c == 0), stop=(c == 7)),
                          waits=w0, inc="t" if c == 7 else None)
            xsl = xt[:, qi * 1024 + eh * 512:qi * 1024 + (eh + 1) * 512]
            ry = r.op("vector", lambda e, yb=yb, xsl=xsl: e.tensor_add(
                xsl, xsl, pbank(yb)), waits=[mm, x_ready], inc="v")
            bank_free[yb] = ry
            last_res = ry
        free[f"atf{sl}"] = mm
        free[f"qfm{sl}"] = mms[-1]
        free[f"qld{sl}"] = tr
    return last_res


def emit_rms_head(r, dst, src, gwb, rinv_scratch, stats16, waits=(), eps=1e-6,
                  nh=16, hd=64, mean=True):
    sq = rinv_scratch
    ss = stats16[:, 0:nh]
    rr = stats16[:, nh:2 * nh]
    e1 = r.op("scalar", lambda e: e.activation(out=sq[:, :], in_=src,
                                               func=AF.Square),
              waits=waits, inc="a")
    e2 = r.op("vector", lambda e: e.tensor_reduce(
        ss, sq[:, :].rearrange("p (h d) -> p h d", d=hd), axis=AX.X,
        op=ALU.add), waits=[e1], inc="v")
    e3 = r.op("vector", lambda e: e.tensor_scalar(
        out=ss, in0=ss, scalar1=(1.0 / hd) if mean else 1.0, scalar2=eps,
        op0=ALU.mult, op1=ALU.add), waits=[e2], inc="v")
    e4 = r.op("scalar", lambda e: e.activation(out=rr, in_=ss, func=AF.Sqrt),
              waits=[e3], inc="a")
    e5 = r.op("vector", lambda e: e.reciprocal(rr, rr), waits=[e4], inc="v")
    rb = bcast_last0(rr, hd)
    e6 = r.op("vector", lambda e: e.tensor_mul(
        sq[:, :].rearrange("p (h d) -> p h d", d=hd),
        src.rearrange("p (h d) -> p h d", d=hd), rb), waits=[e5], inc="v")
    e7 = r.op("vector", lambda e: e.tensor_mul(dst, sq[:, :], gwb),
              waits=[e6], inc="v")
    return e7


def bcast_last0(ap, n):
    import concourse.bass as _bass
    a = [list(d) for d in ap.ap]
    return _bass.AP(ap.tensor, ap.offset, [a[0], a[1], [0, n]])



def emit_s3(r, xt, x_ready, xfull_d, xown_d, wqkv_d, wproj_d, bproj_d,
            qgw_d, kgw_d, cq_d, sq_d, ck_d, sk_d, masks_d,
            tag="s3", idn=None, idn_ev=None, fused=None):
    NT = 64
    NQ = 16
    MAXSEG = 16

    if idn is None:
        idn, idn_ev = make_id128(r, BF16, tag + "_id")
    pb = r.psum(tag + "_pb", [128, 4096], F32)
    pbank = lambda b, w=512: pb[:, b * 512:b * 512 + w]
    ptr = pb.bitcast(BF16)
    bank_free = {b: None for b in range(8)}
    ln = LN(r, tag + "_ln", 4)

    kfm_d = r.nc.dram_tensor(tag + "_kfm", [64, 128, 8, 128], BF16).ap()
    vtm_d = r.nc.dram_tensor(tag + "_vtm", [8192, 1024], BF16).ap()
    qtm_d = r.nc.dram_tensor(tag + "_qtm", [128, NQ * 1024], BF16).ap()

    wbuf = r.sbuf(tag + "_wb", [128, 8 * 2048], BF16)
    wv_ = lambda c, o, wdt=1024: wbuf[:, c * 2048 + o:c * 2048 + o + wdt]
    e_wkv = r.dma("gpsimd", out=wbuf[:, :].rearrange("p (c f) -> p c f", c=8),
                  in_=wqkv_d[:, 1024:3072].rearrange("(c p) f -> p c f",
                                                     p=128), inc="dG")
    oseg = r.sbuf(tag + "_os", [128, 16 * 256], BF16)
    qgw = oseg[:, 2048:3072]
    kgw = oseg[:, 3072:4096]
    bob = r.sbuf(tag + "_bob", [128, 1024], F32)
    grow = r.sbuf(tag + "_gr", [1, 3 * 1024], F32)
    onesf = r.sbuf(tag + "_onf", [1, 128], F32)
    e_g1 = r.dma("sync", out=grow[:, 0:1024], in_=qgw_d, inc="dS")
    e_g2 = r.dma("sync", out=grow[:, 1024:2048], in_=kgw_d, inc="dS")
    e_g3 = r.dma("sync", out=grow[:, 2048:3072], in_=bproj_d, inc="dS")
    eS = e_g3
    ev_onf = r.op("gpsimd", lambda e: e.memset(onesf[:, :], 1.0), inc="g_const")
    gevs = []
    for i, dst in enumerate([qgw, kgw, bob]):
        for hf in range(2):
            b = 4 + hf
            mmg = r.op("tensor", lambda e, i=i, hf=hf, b=b: e.matmul(
                pbank(b), onesf[:, :],
                grow[:, i * 1024 + hf * 512:i * 1024 + (hf + 1) * 512],
                start=True, stop=True),
                waits=[eS, ev_onf, bank_free[b]], inc="t")
            ge = r.op("vector", lambda e, dst=dst, hf=hf, b=b: e.tensor_copy(
                dst[:, hf * 512:(hf + 1) * 512], pbank(b)),
                waits=[mmg], inc="v")
            bank_free[b] = ge
            gevs.append(ge)
    e_gw = gevs[-1]

    cks = [r.sbuf(tag + f"_ck{p}", [128, 512], F32) for p in range(2)]
    sks = [r.sbuf(tag + f"_sk{p}", [128, 512], F32) for p in range(2)]
    sem_tk = [r.sem(tag + f"_tk{p}") for p in range(2)]
    if fused is None:
        xrt = [r.sbuf(tag + f"_xr{p}", [128, 1024], F32) for p in range(2)]
    else:
        qldf = [r.sbuf(tag + f"_qd{p}", [128, 1024], BF16) for p in range(2)]
    sem_xr = [r.sem(tag + f"_xr{p}") for p in range(2)]
    sem_qg = [r.sem(tag + f"_qg{p}") for p in range(2)]
    xnt = [r.sbuf(tag + f"_xn{p}", [128, 1024], BF16) for p in range(2)]
    xnf = [r.sbuf(tag + f"_xf{p}", [128, 1024], BF16) for p in range(2)]
    kraw = [oseg[:, 0:1024], oseg[:, 1024:2048]]
    arena = r.sbuf(tag + "_arena", [128, 8192], BF16)
    krms = [arena[:, 0:1024], arena[:, 1024:2048]]
    kslab = [arena[:, 2048:3072], arena[:, 3072:4096]]
    qraw = [arena[:, 4096:5120], arena[:, 5120:6144]]
    qrms = [arena[:, 6144:7168], arena[:, 7168:8192]]
    rtmp = r.sbuf(tag + "_rt", [128, 1024], BF16)
    stats16 = r.sbuf(tag + "_s16", [128, 48], F32)
    sem_ks = [r.sem(tag + f"_ks{p}") for p in range(2)]
    sem_vs = [r.sem(tag + f"_vs{p}") for p in range(2)]
    sem_qs = [r.sem(tag + f"_qs{p}") for p in range(2)]

    free = {}
    for it in range(NT):
        sl = it % 2
        if fused is None:
            exr = r.dma("sync", out=xrt[sl][:, :],
                        in_=xfull_d[it * 128:(it + 1) * 128, :],
                        waits=[free.get(f"xrt{sl}")], inc=sem_xr[sl])
            lev = ln.tile(it % 4, xrt[sl][:, :], xnt[sl][:, :],
                          x_ready=[exr, free.get(f"xnt{sl}")])
            free[f"xrt{sl}"] = lev
        else:
            lev = r.dma("sync", out=xnt[sl][:, :],
                        in_=fused["xnf_d"][it * 128:(it + 1) * 128, :],
                        waits=[free.get(f"xnt{sl}"), fused["ag_ev"]],
                        inc=sem_xr[sl])
        trs = None
        for c in range(8):
            b = 6 + (c % 2)
            tr = r.op("tensor", lambda e, b=b, c=c, sl=sl: e.transpose(
                ptr[:, b * 1024:b * 1024 + 128],
                xnt[sl][:, c * 128:(c + 1) * 128], idn[:, :]),
                waits=[lev, idn_ev, bank_free[b], free.get(f"xnf{sl}")],
                inc="t")
            ev = r.op("scalar", lambda e, b=b, c=c, sl=sl: e.copy(
                xnf[sl][:, c * 128:(c + 1) * 128],
                ptr[:, b * 1024:b * 1024 + 128]), waits=[tr], inc="a")
            bank_free[b] = ev
            trs = ev
            free[f"xnt{sl}"] = tr
        mm = None
        for hf in range(2):
            kb = hf
            for c in range(8):
                w0 = [bank_free[kb], trs, e_wkv] if c == 0 else []
                mm = r.op("tensor", lambda e, kb=kb, c=c, hf=hf, sl=sl:
                          e.matmul(pbank(kb),
                                   xnf[sl][:, c * 128:(c + 1) * 128],
                                   wv_(c, hf * 512, 512),
                                   start=(c == 0), stop=(c == 7)),
                          waits=w0, inc="t" if c == 7 else None)
            ek = r.op("vector", lambda e, kb=kb, sl=sl, hf=hf: e.tensor_copy(
                kraw[sl][:, hf * 512:(hf + 1) * 512], pbank(kb)),
                waits=[mm, free.get(f"kraw{sl}")] if hf == 0 else [mm],
                inc="v")
            bank_free[kb] = ek
        erm = emit_rms_head(r, krms[sl][:, :], kraw[sl][:, :], kgw[:, :],
                            rtmp[:, :], stats16[:, 0:32],
                            waits=[ek, e_gw, free.get(f"krms{sl}"),
                                   free.get("rtmp")])
        free[f"kraw{sl}"] = erm
        etk = r.dma("sync", out=cks[sl][:, :],
                    in_=ck_d[it * 128:(it + 1) * 128, :],
                    waits=[free.get(f"cks{sl}")], inc=sem_tk[sl])
        etk2 = r.dma("sync", out=sks[sl][:, :],
                     in_=sk_d[it * 128:(it + 1) * 128, :],
                     waits=[free.get(f"cks{sl}")], inc=sem_tk[sl])
        erp = emit_rope_gpt(r, kraw[sl][:, :], krms[sl][:, :], cks[sl][:, :],
                            sks[sl][:, :], 1024, rtmp[:, :],
                            waits=[erm, etk2, eS],
                            extra=[free.get(f"kraw{sl}")])
        free[f"krms{sl}"] = erp
        free[f"cks{sl}"] = erp
        free["rtmp"] = erp
        ks_ev = None
        ktr = None
        for j in range(8):
            b = 6 + (j % 2)
            ktr = r.op("tensor", lambda e, b=b, j=j, sl=sl: e.transpose(
                ptr[:, b * 1024:b * 1024 + 128],
                kraw[sl][:, j * 128:(j + 1) * 128], idn[:, :]),
                waits=[erp, bank_free[b], free.get(f"kslab{sl}")], inc="t")
            ev = r.op("scalar", lambda e, b=b, j=j, sl=sl: e.copy(
                kslab[sl][:, j * 128:(j + 1) * 128],
                ptr[:, b * 1024:b * 1024 + 128]), waits=[ktr], inc="a")
            bank_free[b] = ev
            ks_ev = ev
        free[f"kraw{sl}"] = ktr
        eks = r.dma("sync", out=kfm_d[it].rearrange("p c j -> p (c j)"),
                    in_=kslab[sl][:, :], waits=[ks_ev], inc=sem_ks[sl])
        free[f"kslab{sl}"] = eks
        for hf in range(2):
            vb = 2 + hf
            for c in range(8):
                w0 = [bank_free[vb], trs] if c == 0 else []
                mm = r.op("tensor", lambda e, vb=vb, c=c, hf=hf, sl=sl:
                          e.matmul(pbank(vb),
                                   xnf[sl][:, c * 128:(c + 1) * 128],
                                   wv_(c, 1024 + hf * 512, 512),
                                   start=(c == 0), stop=(c == 7)),
                          waits=w0, inc="t" if c == 7 else None)
            evv = r.op("vector", lambda e, vb=vb, sl=sl, hf=hf: e.tensor_copy(
                xnt[sl][:, hf * 512:(hf + 1) * 512], pbank(vb)),
                waits=[mm, free.get(f"xnt{sl}")] if hf == 0 else [mm],
                inc="v")
            bank_free[vb] = evv
        free[f"xnf{sl}"] = mm
        evs = r.dma("sync", out=vtm_d[it * 128:(it + 1) * 128, :],
                    in_=xnt[sl][:, :], waits=[evv], inc=sem_vs[sl])
        free[f"xnt{sl}"] = evs
    phaseA_mm = mm

    e_wq = r.dma("gpsimd", out=wbuf[:, :].rearrange(
        "p (c f) -> p c f", c=8)[:, :, 0:1024],
        in_=wqkv_d[:, 0:1024].rearrange("(c p) f -> p c f", p=128),
        waits=[phaseA_mm], inc="dG")
    cqs = cks
    sqs = sks
    free["cqs0"] = free.get("cks0")
    free["cqs1"] = free.get("cks1")
    sem_tq = [r.sem(tag + f"_tq{p}") for p in range(2)]
    for j in range(NQ):
        sl = j % 2
        if fused is None:
            lev = ln.tile(j % 4, xt[:, j * 1024:(j + 1) * 1024], xnt[sl][:, :],
                          x_ready=[x_ready, free.get(f"xnt{sl}")])
        else:
            lev = r.op("gpsimd", lambda e, sl=sl, j=j: e.indirect_dma_start(
                out=xnt[sl][:, :], out_offset=None, in_=fused["xnf_d"],
                in_offset=IndirectOffsetOnAxis(
                    fused["qidx"][:, j:j + 1], 0)),
                waits=[free.get(f"xnt{sl}"), fused["ag_ev"],
                       fused["qidx_ev"]], inc=sem_qg[sl], inc_k=16, dma=True)
        trs = None
        for c in range(8):
            b = 6 + (c % 2)
            tr = r.op("tensor", lambda e, b=b, c=c, sl=sl: e.transpose(
                ptr[:, b * 1024:b * 1024 + 128],
                xnt[sl][:, c * 128:(c + 1) * 128], idn[:, :]),
                waits=[lev, bank_free[b], free.get(f"xnf{sl}")], inc="t")
            ev = r.op("scalar", lambda e, b=b, c=c, sl=sl: e.copy(
                xnf[sl][:, c * 128:(c + 1) * 128],
                ptr[:, b * 1024:b * 1024 + 128]), waits=[tr], inc="a")
            bank_free[b] = ev
            trs = ev
            free[f"xnt{sl}"] = tr
        mm = None
        for hf in range(2):
            qb = hf
            for c in range(8):
                w0 = [bank_free[qb], trs, e_wq] if c == 0 else []
                mm = r.op("tensor", lambda e, qb=qb, c=c, hf=hf, sl=sl:
                          e.matmul(pbank(qb),
                                   xnf[sl][:, c * 128:(c + 1) * 128],
                                   wv_(c, hf * 512, 512),
                                   start=(c == 0), stop=(c == 7)),
                          waits=w0, inc="t" if c == 7 else None)
            eq = r.op("vector", lambda e, qb=qb, sl=sl, hf=hf: e.tensor_copy(
                qraw[sl][:, hf * 512:(hf + 1) * 512], pbank(qb)),
                waits=[mm, free.get(f"qraw{sl}")] if hf == 0 else [mm],
                inc="v")
            bank_free[qb] = eq
        free[f"xnf{sl}"] = mm
        erm = emit_rms_head(r, qrms[sl][:, :], qraw[sl][:, :], qgw[:, :],
                            rtmp[:, :], stats16[:, 0:32],
                            waits=[eq, free.get(f"qrms{sl}"),
                                   free.get("rtmp")])
        free[f"qraw{sl}"] = erm
        etq = r.dma("sync", out=cqs[sl][:, :],
                    in_=cq_d[j * 128:(j + 1) * 128, :],
                    waits=[free.get(f"cqs{sl}")], inc=sem_tq[sl])
        etq2 = r.dma("sync", out=sqs[sl][:, :],
                     in_=sq_d[j * 128:(j + 1) * 128, :],
                     waits=[free.get(f"cqs{sl}")], inc=sem_tq[sl])
        erp = emit_rope_gpt(r, qraw[sl][:, :], qrms[sl][:, :], cqs[sl][:, :],
                            sqs[sl][:, :], 1024, rtmp[:, :],
                            waits=[erm, etq2], extra=[free.get(f"qraw{sl}")])
        free[f"qrms{sl}"] = erp
        free[f"cqs{sl}"] = erp
        free["rtmp"] = erp
        eqs_ = r.dma("scalar", out=qtm_d[:, j * 1024:(j + 1) * 1024],
                     in_=qraw[sl][:, :], waits=[erp], inc=sem_qs[sl])
        free[f"qraw{sl}"] = eqs_
        qstore = eqs_
    e_wp = r.dma("gpsimd", out=wbuf[:, :].rearrange(
        "p (c f) -> p c f", c=8)[:, :, 1024:2048],
        in_=wproj_d.rearrange("(c p) f -> p c f", p=128),
        waits=[mm], inc="dG")

    pexp = [arena[:, 0:2048], arena[:, 2048:4096]]
    ptts = [arena[:, 4096:6144], arena[:, 6144:8192]]
    free["pexp0"] = free.get("kslab0")
    free["pexp1"] = free.get("kslab1")
    free["ptts0"] = free.get("qrms0") or qstore
    free["ptts1"] = free.get("qrms1") or qstore
    arena_guards = [free.get("krms0"), free.get("krms1"),
                    free.get("qraw0"), free.get("qraw1"), qstore]
    kstot = [Event(sem_ks[0], sem_ks[0].n), Event(sem_ks[1], sem_ks[1].n)]
    vstot = [Event(sem_vs[0], sem_vs[0].n), Event(sem_vs[1], sem_vs[1].n)]
    qstot = [Event(sem_qs[0], sem_qs[0].n), Event(sem_qs[1], sem_qs[1].n)]
    if fused is None:
        qld = [xrt[0].bitcast(BF16)[:, 0:1024],
               xrt[1].bitcast(BF16)[:, 0:1024]]
        free["qld0"] = free.get("xrt0")
        free["qld1"] = free.get("xrt1")
    else:
        qld = qldf
    qfm = [r.sbuf(tag + f"_qf{p}", [128, 1024], BF16) for p in range(2)]
    sem_ql = [r.sem(tag + f"_ql{p}") for p in range(2)]
    kseg = [r.sbuf(tag + f"_kg{p}", [128, 8 * 512], BF16) for p in range(2)]
    vseg = [r.sbuf(tag + f"_vg{p}", [128, 4 * 1024], BF16) for p in range(2)]
    sem_kg = [r.sem(tag + f"_kg{p}") for p in range(2)]
    sem_vg = [r.sem(tag + f"_vg{p}") for p in range(2)]
    mskb = xnt
    free["mskb0"] = free.get("xnt0")
    free["mskb1"] = free.get("xnt1")
    sem_mk = [r.sem(tag + f"_mk{p}") for p in range(2)]
    free["oseg"] = qstore
    msegs = r.sbuf(tag + "_ms", [128, MAXSEG * 4], F32)
    zsegs = r.sbuf(tag + "_zs", [128, MAXSEG * 4], F32)
    alph = r.sbuf(tag + "_al", [128, MAXSEG * 4], F32)
    azs = r.sbuf(tag + "_az", [128, MAXSEG * 4], F32)
    stats = r.sbuf(tag + "_sst", [128, 8], F32)
    ofin = r.sbuf(tag + "_of", [128, 4 * 64], F32)
    attn = r.sbuf(tag + "_attn", [128, 1024], BF16)
    atf = xnf
    free["atf0"] = free.get("xnf0")
    free["atf1"] = free.get("xnf1")

    last_res = None
    for j in range(NQ):
        sl = j % 2
        nseg = min(j + 2, MAXSEG)
        eql = r.dma("scalar", out=qld[sl][:, :],
                    in_=qtm_d[:, j * 1024:(j + 1) * 1024],
                    waits=[qstot[0], qstot[1], free.get(f"qld{sl}")],
                    inc=sem_ql[sl])
        emk = r.dma("scalar", out=mskb[sl][:, :],
                    in_=masks_d[j].rearrange("p f -> p f"),
                    waits=[free.get(f"mskb{sl}")], inc=sem_mk[sl])
        qtr = None
        for jj in range(8):
            b = 6 + (jj % 2)
            tr = r.op("tensor", lambda e, b=b, jj=jj, sl=sl: e.transpose(
                ptr[:, b * 1024:b * 1024 + 128],
                qld[sl][:, jj * 128:(jj + 1) * 128], idn[:, :]),
                waits=[eql, bank_free[b], free.get(f"qfm{sl}")], inc="t")
            ev = r.op("scalar", lambda e, b=b, jj=jj, sl=sl: e.copy(
                qfm[sl][:, jj * 128:(jj + 1) * 128],
                ptr[:, b * 1024:b * 1024 + 128]), waits=[tr], inc="a")
            bank_free[b] = ev
            qtr = ev
        free[f"qld{sl}"] = tr
        for hg in range(4):
            oe = None
            for sg in range(nseg):
                sgl = sg % 2
                ekg = r.dma("gpsimd", out=kseg[sgl][:, :].rearrange(
                    "p (c t jj) -> p c t jj", c=8, t=4),
                    in_=kfm_d[sg * 4:(sg + 1) * 4].rearrange(
                        "t p c jj -> p c t jj"),
                    waits=[free.get(f"kseg{sgl}"), kstot[0], kstot[1]],
                    inc=sem_kg[sgl])
                evg = r.dma("gpsimd", out=vseg[sgl][:, :].rearrange(
                    "p (t f) -> p t f", t=4),
                    in_=vtm_d[sg * 512:(sg + 1) * 512, :].rearrange(
                        "(t p) f -> p t f", p=128),
                    waits=[free.get(f"vseg{sgl}"), vstot[0], vstot[1]],
                    inc=sem_vg[sgl])
                mms = []
                for r4 in range(4):
                    h = hg * 4 + r4
                    hj = h // 2
                    hb = (h % 2) * 64
                    mm = r.op("tensor", lambda e, r4=r4, hj=hj, hb=hb, sl=sl,
                              sgl=sgl: e.matmul(
                                  pbank(r4),
                                  qfm[sl][:, hj * 128:(hj + 1) * 128]
                                  [hb:hb + 64, :],
                                  kseg[sgl][:, hj * 512:(hj + 1) * 512]
                                  [hb:hb + 64, :],
                                  start=True, stop=True),
                              waits=[qtr, ekg, bank_free[r4]], inc="t")
                    mms.append(mm)
                if sg >= nseg - 2:
                    mh = (sg - (nseg - 2)) * 512
                    nm = []
                    for r4 in range(4):
                        me = r.op("vector", lambda e, r4=r4, mh=mh, sl=sl:
                                  e.tensor_add(pbank(r4), pbank(r4),
                                               mskb[sl][:, mh:mh + 512]),
                                  waits=[mms[r4], emk], inc="v")
                        nm.append(me)
                    mms = nm
                e1 = r.op("vector", lambda e, sg=sg: e.reduce_max(
                    msegs[:, sg * 4:sg * 4 + 4],
                    pb[:, 0:2048].rearrange("p (h k) -> p h k", h=4),
                    axis=AX.X, negate=True),
                    waits=mms, inc="v")
                e2s = []
                for r4 in range(4):
                    e2 = r.op("scalar", lambda e, r4=r4, sg=sg, sgl=sgl:
                              e.activation(
                                  out=pexp[sgl][:, r4 * 512:(r4 + 1) * 512],
                                  in_=pbank(r4), func=AF.Exp,
                                  bias=msegs[:, sg * 4 + r4:sg * 4 + r4 + 1],
                                  scale=1.0,
                                  accum_out=zsegs[:, sg * 4 + r4:
                                                  sg * 4 + r4 + 1]),
                              waits=[e1] + _fl(free.get(f"pexp{sgl}"))
                              + arena_guards, inc="a")
                    e2s.append(e2)
                mmpv = {}
                tdfin = {}
                for r4 in range(4):
                    for cc in range(4):
                        tdfin[cc] = r.dma(
                            "scalar",
                            out=ptts[sgl][:, r4 * 512 + cc * 128:
                                          r4 * 512 + (cc + 1) * 128],
                            in_=pexp[sgl][:, r4 * 512 + cc * 128:
                                          r4 * 512 + (cc + 1) * 128],
                            transpose=True,
                            waits=[e2s[r4], free.get(f"ptts{sgl}")],
                            inc=f"dT{cc}")
                tdq = [tdfin[0], tdfin[1], tdfin[2], tdfin[3]]
                for r4 in range(4):
                    h = hg * 4 + r4
                    for cc in range(4):
                        w0 = [tdfin[cc]]
                        if cc == 0:
                            w0 += [evg]
                        mm = r.op("tensor", lambda e, r4=r4, cc=cc, sgl=sgl,
                                  h=h: e.matmul(
                                      pbank(r4, 64),
                                      ptts[sgl][:, r4 * 512 + cc * 128:
                                                r4 * 512 + (cc + 1) * 128],
                                      vseg[sgl][:, cc * 1024 + h * 64:
                                                cc * 1024 + (h + 1) * 64],
                                      start=(cc == 0), stop=(cc == 3)),
                                  waits=w0, inc="t" if cc == 3 else None)
                        if cc == 3:
                            mmpv[r4] = mm
                free[f"ptts{sgl}"] = mm
                free[f"pexp{sgl}"] = tdq
                free[f"kseg{sgl}"] = mms[-1] if sg < nseg - 2 else e1
                free[f"vseg{sgl}"] = mm
                oe = None
                for r4 in range(4):
                    oe = r.op("vector", lambda e, sg=sg, r4=r4: e.tensor_copy(
                        oseg[:, sg * 256 + r4 * 64:sg * 256 + (r4 + 1) * 64],
                        pbank(r4, 64)),
                        waits=[mmpv[r4], free.get("oseg")], inc="v")
                    bank_free[r4] = oe
            free[f"mskb{sl}"] = e1
            c1 = r.op("vector", lambda e, nseg=nseg: e.tensor_reduce(
                stats[:, 0:4],
                msegs[:, 0:nseg * 4].rearrange("p (s h) -> p h s", h=4),
                axis=AX.X, op=ALU.min), waits=[oe], inc="v")
            attn_ev = None
            for r4 in range(4):
                h = hg * 4 + r4
                c2 = r.op("scalar", lambda e, r4=r4, nseg=nseg: e.activation(
                    out=alph[:, r4 * MAXSEG:r4 * MAXSEG + nseg],
                    in_=msegs[:, 0:nseg * 4].rearrange(
                        "p (s h) -> p h s", h=4)[:, r4],
                    func=AF.Exp, bias=stats[:, r4:r4 + 1], scale=-1.0),
                    waits=[c1], inc="a")
                c3 = r.op("vector", lambda e, r4=r4, nseg=nseg: e.tensor_mul(
                    azs[:, r4 * MAXSEG:r4 * MAXSEG + nseg],
                    alph[:, r4 * MAXSEG:r4 * MAXSEG + nseg],
                    zsegs[:, 0:nseg * 4].rearrange(
                        "p (s h) -> p h s", h=4)[:, r4]),
                    waits=[c2], inc="v")
                c4 = r.op("vector", lambda e, r4=r4, nseg=nseg: e.reduce_sum(
                    stats[:, 4 + r4:5 + r4],
                    azs[:, r4 * MAXSEG:r4 * MAXSEG + nseg], axis=AX.X),
                    waits=[c3], inc="v")
                co = None
                for sg2 in range(nseg):
                    osl = oseg[:, sg2 * 256 + r4 * 64:sg2 * 256 + (r4 + 1) * 64]
                    if sg2 == 0:
                        co = r.op("vector", lambda e, osl=osl, r4=r4:
                                  e.tensor_scalar(
                                      out=ofin[:, r4 * 64:(r4 + 1) * 64],
                                      in0=osl,
                                      scalar1=alph[:, r4 * MAXSEG:
                                                   r4 * MAXSEG + 1],
                                      scalar2=None, op0=ALU.mult),
                                  waits=[c2, free.get("ofin")], inc="v")
                    else:
                        co = r.op("vector", lambda e, osl=osl, r4=r4, sg2=sg2:
                                  e.scalar_tensor_tensor(
                                      out=ofin[:, r4 * 64:(r4 + 1) * 64],
                                      in0=osl,
                                      scalar=alph[:, r4 * MAXSEG + sg2:
                                                  r4 * MAXSEG + sg2 + 1],
                                      in1=ofin[:, r4 * 64:(r4 + 1) * 64],
                                      op0=ALU.mult, op1=ALU.add),
                                  waits=[co], inc="v")
                c5 = r.op("vector", lambda e, r4=r4: e.reciprocal(
                    stats[:, 4 + r4:5 + r4], stats[:, 4 + r4:5 + r4]),
                    waits=[c4], inc="v")
                ca = r.op("vector", lambda e, h=h, r4=r4: e.tensor_scalar(
                    out=attn[:, h * 64:(h + 1) * 64],
                    in0=ofin[:, r4 * 64:(r4 + 1) * 64],
                    scalar1=stats[:, 4 + r4:5 + r4], scalar2=None,
                    op0=ALU.mult),
                    waits=[co, c5, free.get("attn")], inc="v")
                attn_ev = ca
            free["ofin"] = attn_ev
            free["oseg"] = attn_ev
        atr = None
        for jj in range(8):
            b = 6 + (jj % 2)
            tr = r.op("tensor", lambda e, b=b, jj=jj, sl=sl: e.transpose(
                ptr[:, b * 1024:b * 1024 + 128],
                attn[:, jj * 128:(jj + 1) * 128], idn[:, :]),
                waits=[attn_ev, bank_free[b], free.get(f"atf{sl}")], inc="t")
            ec = r.op("scalar", lambda e, b=b, jj=jj, sl=sl: e.copy(
                atf[sl][:, jj * 128:(jj + 1) * 128],
                ptr[:, b * 1024:b * 1024 + 128]), waits=[tr], inc="a")
            bank_free[b] = ec
            atr = ec
        free["attn"] = tr
        ysb = None
        for eh in range(2):
            yb = 2 + eh
            mm = None
            for c in range(8):
                w0 = [atr, bank_free[yb], e_wp] if c == 0 else []
                mm = r.op("tensor", lambda e, yb=yb, c=c, eh=eh, sl=sl:
                          e.matmul(pbank(yb),
                                   atf[sl][:, c * 128:(c + 1) * 128],
                                   wv_(c, 1024 + eh * 512, 512),
                                   start=(c == 0), stop=(c == 7)),
                          waits=w0, inc="t" if c == 7 else None)
            if fused is None:
                xsl = xt[:, j * 1024 + eh * 512:j * 1024 + (eh + 1) * 512]
                ry = r.op("vector", lambda e, yb=yb, xsl=xsl: e.tensor_add(
                    xsl, xsl, pbank(yb)), waits=[mm, x_ready], inc="v")
                ry2 = r.op("vector", lambda e, xsl=xsl, eh=eh: e.tensor_add(
                    xsl, xsl, bob[:, eh * 512:(eh + 1) * 512]),
                    waits=[ry, e_gw], inc="v")
                bank_free[yb] = ry2
                last_res = ry2
            else:
                ysl = qld[sl][:, eh * 512:(eh + 1) * 512]
                ry = r.op("vector", lambda e, yb=yb, ysl=ysl, eh=eh:
                          e.tensor_add(ysl, pbank(yb),
                                       bob[:, eh * 512:(eh + 1) * 512]),
                          waits=[mm, e_gw, free.get(f"qld{sl}")], inc="v")
                bank_free[yb] = ry
                ysb = ry
        if fused is not None:
            eys = r.dma("scalar", out=fused["ybuf_d"][j * 128:(j + 1) * 128, :],
                        in_=qld[sl][:, :], waits=[ysb], inc=sem_ql[sl])
            free[f"qld{sl}"] = eys
            last_res = eys
        free[f"atf{sl}"] = mm
        free[f"qfm{sl}"] = mms[-1] if True else None
    return last_res




def emit_s7a(r, xfull_d, wqT_d, wkT_d, wvT_d, wg_d, bg_d, wl_d, bl_d, y_d,
             tag="s7", idn=None, idn_ev=None, fused=None):
    NT = 64

    if idn is None:
        idn, idn_ev = make_id128(r, BF16, tag + "_id")
    idnf = r.sbuf(tag + "_idf", [128, 128], F32)
    ei0 = r.op("gpsimd", lambda e: e.memset(idnf[:, :], 0), inc="g_const")
    idnf_ev = r.op("gpsimd", lambda e: e.affine_select(
        out=idnf[:, :], in_=idnf[:, :], compare_op=ALU.not_equal,
        fill=1.0, base=0, pattern=[[-1, 128]], channel_multiplier=1),
        waits=[ei0], inc="g_const")
    trim = r.sbuf(tag + "_trim", [128, 128], F32)
    et0 = r.op("gpsimd", lambda e: e.memset(trim[:, :], 0), inc="g_const")
    trim_ev = r.op("gpsimd", lambda e: e.affine_select(
        out=trim[:, :], in_=trim[:, :], compare_op=ALU.is_ge,
        fill=-1e30, base=0, pattern=[[1, 128]], channel_multiplier=-1),
        waits=[et0], inc="g_const")

    pb = r.psum(tag + "_pb", [128, 4096], F32)
    pbank = lambda b, w=512: pb[:, b * 512:b * 512 + w]
    ptr = pb.bitcast(BF16)
    bank_free = {b: None for b in range(8)}
    ln = LN(r, tag + "_ln", 4)

    wq = r.sbuf(tag + "_wq", [128, 8 * 256], BF16)
    wk = r.sbuf(tag + "_wk", [128, 8 * 256], BF16)
    wv = r.sbuf(tag + "_wv", [128, 8 * 256], BF16)
    e_w = None
    for dst, srcd in [(wq, wqT_d), (wk, wkT_d), (wv, wvT_d)]:
        e_w = r.dma("gpsimd", out=dst[:, :].rearrange("p (c f) -> p c f", c=8),
                    in_=srcd.rearrange("(c p) f -> p c f", p=128), inc="dG")
    grow = r.sbuf(tag + "_gr", [1, 16], F32)
    gb = r.sbuf(tag + "_gb", [128, 16], F32)
    onesf = r.sbuf(tag + "_onf", [1, 128], F32)
    e_g = None
    for i, srcd in enumerate([wg_d, bg_d, wl_d, bl_d]):
        e_g = r.dma("sync", out=grow[:, i * 4:(i + 1) * 4], in_=srcd, inc="dS")
    ev_onf = r.op("gpsimd", lambda e: e.memset(onesf[:, :], 1.0), inc="g_const")
    mmg = r.op("tensor", lambda e: e.matmul(pbank(5, 16), onesf[:, :],
                                            grow[:, :], start=True, stop=True),
               waits=[e_g, ev_onf], inc="t")
    e_gb = r.op("vector", lambda e: e.tensor_copy(gb[:, :], pbank(5, 16)),
                waits=[mmg], inc="v")
    bank_free[5] = e_gb

    qtm = r.sbuf(tag + "_q", [128, NT * 256], BF16)
    ktm = r.sbuf(tag + "_k", [128, NT * 256], BF16)
    vtm = r.sbuf(tag + "_v", [128, NT * 256], BF16)
    ysc = [r.sbuf(tag + f"_y{p}", [128, 256], BF16) for p in range(2)]
    latm = r.sbuf(tag + "_la", [128, NT * 4], F32)
    utm = r.sbuf(tag + "_u", [128, NT * 4], F32)
    if fused is None:
        xrt = [r.sbuf(tag + f"_xr{p}", [128, 1024], F32) for p in range(2)]
    sem_xr = [r.sem(tag + f"_xr{p}") for p in range(2)]
    xnt = [r.sbuf(tag + f"_xn{p}", [128, 1024], BF16) for p in range(2)]
    xnf = [r.sbuf(tag + f"_xf{p}", [128, 1024], BF16) for p in range(2)]
    sq64 = r.sbuf(tag + "_sq", [128, 256], BF16)
    st16 = r.sbuf(tag + "_st", [128, 24], F32)
    sem_ys = [r.sem(tag + f"_ys{p}") for p in range(2)]

    free = {}
    for it in range(NT):
        sl = it % 2
        if fused is None:
            exr = r.dma("sync", out=xrt[sl][:, :],
                        in_=xfull_d[it * 128:(it + 1) * 128, :],
                        waits=[free.get(f"xrt{sl}")], inc=sem_xr[sl])
            lev = ln.tile(it % 4, xrt[sl][:, :], xnt[sl][:, :],
                          x_ready=[exr, free.get(f"xnt{sl}")])
            free[f"xrt{sl}"] = lev
        else:
            lev = r.dma("sync", out=xnt[sl][:, :],
                        in_=fused["xnf_d"][it * 128:(it + 1) * 128, :],
                        waits=[free.get(f"xnt{sl}"), fused["ag_ev"]],
                        inc=sem_xr[sl])
        trs = None
        for c in range(8):
            b = 6 + (c % 2)
            tr = r.op("tensor", lambda e, b=b, c=c, sl=sl: e.transpose(
                ptr[:, b * 1024:b * 1024 + 128],
                xnt[sl][:, c * 128:(c + 1) * 128], idn[:, :]),
                waits=[lev, idn_ev, bank_free[b], free.get(f"xnf{sl}")],
                inc="t")
            ev = r.op("scalar", lambda e, b=b, c=c, sl=sl: e.copy(
                xnf[sl][:, c * 128:(c + 1) * 128],
                ptr[:, b * 1024:b * 1024 + 128]), waits=[tr], inc="a")
            bank_free[b] = ev
            trs = ev
            free[f"xnt{sl}"] = tr
        evs = {}
        for gi, (wt, dstname) in enumerate([(wq, "q"), (wk, "k"), (wv, "v")]):
            gbk = gi
            mm = None
            for c in range(8):
                w0 = [bank_free[gbk], trs, e_w] if c == 0 else []
                mm = r.op("tensor", lambda e, gbk=gbk, c=c, sl=sl, wt=wt:
                          e.matmul(pbank(gbk, 256),
                                   xnf[sl][:, c * 128:(c + 1) * 128],
                                   wt[:, c * 256:(c + 1) * 256],
                                   start=(c == 0), stop=(c == 7)),
                          waits=w0, inc="t" if c == 7 else None)
            evs[dstname] = mm
        free[f"xnf{sl}"] = evs["v"]
        eq = r.op("vector", lambda e, it=it: e.tensor_copy(
            qtm[:, it * 256:(it + 1) * 256], pbank(0, 256)),
            waits=[evs["q"]], inc="v")
        bank_free[0] = eq
        evv = r.op("vector", lambda e, it=it: e.tensor_copy(
            vtm[:, it * 256:(it + 1) * 256], pbank(2, 256)),
            waits=[evs["v"]], inc="v")
        bank_free[2] = evv
        e1 = r.op("scalar", lambda e: e.activation(
            out=sq64[:, :], in_=pbank(1, 256), func=AF.Square),
            waits=[evs["k"]], inc="a")
        e2 = r.op("vector", lambda e: e.tensor_reduce(
            st16[:, 0:4], sq64[:, :].rearrange("p (h d) -> p h d", d=64),
            axis=AX.X, op=ALU.add), waits=[e1], inc="v")
        e3 = r.op("scalar", lambda e: e.activation(
            out=st16[:, 4:8], in_=st16[:, 0:4], func=AF.Sqrt),
            waits=[e2], inc="a")
        e4 = r.op("vector", lambda e: e.tensor_scalar_max(
            st16[:, 4:8], st16[:, 4:8], 1e-12), waits=[e3], inc="v")
        e5 = r.op("vector", lambda e: e.reciprocal(st16[:, 8:12],
                                                   st16[:, 4:8]),
                  waits=[e4], inc="v")
        e6 = r.op("vector", lambda e, it=it: e.tensor_mul(
            ktm[:, it * 256:(it + 1) * 256].rearrange(
                "p (h d) -> p h d", d=64),
            pbank(1, 256).rearrange("p (h d) -> p h d", d=64),
            bcast_last0(st16[:, 8:12], 64)), waits=[e5], inc="v")
        bank_free[1] = e6
        e7 = r.op("vector", lambda e, it=it: e.tensor_reduce(
            st16[:, 12:16],
            ktm[:, it * 256:(it + 1) * 256].rearrange(
                "p (h d) -> p h d", d=64), axis=AX.X, op=ALU.add),
            waits=[e6], inc="v")
        e8 = r.op("vector", lambda e: e.tensor_scalar_mul(
            st16[:, 12:16], st16[:, 12:16], 1.0 / 64.0), waits=[e7], inc="v")
        e9 = r.op("vector", lambda e: e.tensor_mul(
            st16[:, 16:20], st16[:, 12:16], gb[:, 0:4]),
            waits=[e8, e_gb], inc="v")
        e10 = r.op("vector", lambda e: e.tensor_add(
            st16[:, 16:20], st16[:, 16:20], gb[:, 4:8]), waits=[e9], inc="v")
        e11 = r.op("scalar", lambda e: e.activation(
            out=st16[:, 16:20], in_=st16[:, 16:20], func=AF.Sigmoid),
            waits=[e10], inc="a")
        e12 = r.op("vector", lambda e: e.tensor_mul(
            st16[:, 20:24], st16[:, 12:16], gb[:, 8:12]), waits=[e8, e11],
            inc="v")
        e13 = r.op("vector", lambda e: e.tensor_add(
            st16[:, 20:24], st16[:, 20:24], gb[:, 12:16]), waits=[e12],
            inc="v")
        e14 = r.op("scalar", lambda e: e.activation(
            out=st16[:, 20:24], in_=st16[:, 20:24], func=AF.Sigmoid),
            waits=[e13], inc="a")
        e15 = r.op("vector", lambda e: e.tensor_scalar(
            out=st16[:, 20:24], in0=st16[:, 20:24], scalar1=-1.0, scalar2=1.0,
            op0=ALU.mult, op1=ALU.add), waits=[e14], inc="v")
        e16 = r.op("vector", lambda e: e.tensor_mul(
            st16[:, 16:20], st16[:, 16:20], st16[:, 20:24]),
            waits=[e15, e11], inc="v")
        e17 = r.op("vector", lambda e, it=it: e.tensor_scalar(
            out=utm[:, it * 4:(it + 1) * 4], in0=st16[:, 16:20],
            scalar1=-1.0, scalar2=1.0, op0=ALU.mult, op1=ALU.add),
            waits=[e16, free.get("utm")], inc="v")
        e18 = r.op("vector", lambda e: e.tensor_scalar_max(
            st16[:, 16:20], st16[:, 16:20], 1e-38), waits=[e16], inc="v")
        e19 = r.op("scalar", lambda e, it=it: e.activation(
            out=latm[:, it * 4:(it + 1) * 4], in_=st16[:, 16:20],
            func=AF.Ln), waits=[e18, free.get("latm")], inc="a")
        kv_ev = (eq, e6, evv, e17, e19)
    phA = kv_ev

    Sst = [r.sbuf(tag + f"_S{p}", [64, 256], F32) for p in range(2)]
    Sbf = r.sbuf(tag + "_Sb", [64, 256], BF16)
    lahm = r.sbuf(tag + "_lah", [4, 128], F32)
    uhm = r.sbuf(tag + "_uh", [4, 128], F32)
    lchm = r.sbuf(tag + "_lch", [4, 128], F32)
    wuhm = r.sbuf(tag + "_wuh", [4, 128], F32)
    zero4 = r.sbuf(tag + "_z4", [4, 128], F32)
    lcrow = r.sbuf(tag + "_lcr", [1, 2 * 512 + 16], F32)
    dtm = r.sbuf(tag + "_dtm", [128, 4], F32)
    wutm = r.sbuf(tag + "_wut", [128, 4], F32)
    qfm_ = r.sbuf(tag + "_qf", [64, 128 * 4], BF16)
    vfm_ = r.sbuf(tag + "_vf", [64, 128 * 4], BF16)
    qdc = r.sbuf(tag + "_qd", [128, 256], BF16)
    dmt = r.sbuf(tag + "_dm", [128, 128], F32)
    gwt = r.sbuf(tag + "_gw", [128, 128], BF16)
    dcol = r.sbuf(tag + "_dc", [64, 4], F32)
    ez = r.op("vector", lambda e: e.memset(zero4[:, :], 0.0), inc="v")
    ezS = r.op("vector", lambda e: e.memset(Sst[0][:, :], 0.0), inc="v")
    sem_lr = r.sem(tag + "_lr")
    sem_cr = r.sem(tag + "_cr")
    ysl_ev = None
    ylast = [None, None]
    Scur = 0
    S_ev = ezS
    for c in range(NT):
        sl = c % 2
        t1 = r.op("tensor", lambda e, c=c: e.transpose(
            pb[0:4, 5 * 512:5 * 512 + 128], latm[:, c * 4:(c + 1) * 4],
            idnf[:, :]), waits=[phA[4], idnf_ev, bank_free[5]], inc="t")
        el = r.op("vector", lambda e: e.tensor_copy(
            lahm[:, :], pb[0:4, 5 * 512:5 * 512 + 128]),
            waits=[t1, free.get("lahm")], inc="v")
        t2 = r.op("tensor", lambda e, c=c: e.transpose(
            pb[0:4, 5 * 512 + 128:5 * 512 + 256], utm[:, c * 4:(c + 1) * 4],
            idnf[:, :]), waits=[phA[3], idnf_ev, el], inc="t")
        eu = r.op("vector", lambda e: e.tensor_copy(
            uhm[:, :], pb[0:4, 5 * 512 + 128:5 * 512 + 256]),
            waits=[t2, free.get("uhm")], inc="v")
        bank_free[5] = eu
        esc = r.op("vector", lambda e: e.tensor_tensor_scan(
            out=lchm[:, :], data0=lahm[:, :], data1=zero4[:, :],
            initial=0.0, op0=ALU.add, op1=ALU.add),
            waits=[el, ez, free.get("lchm")], inc="v")
        free["lahm"] = esc
        ewu2 = r.op("scalar", lambda e: e.activation(
            out=wuhm[:, :], in_=lchm[:, :], func=AF.Exp,
            bias=lchm[:, 127:128], scale=-1.0),
            waits=[esc, free.get("wuhm")], inc="a")
        free["uhm"] = eu
        edh = r.op("scalar", lambda e: e.activation(
            out=lahm[:, :], in_=lchm[:, :], func=AF.Exp),
            waits=[esc, free.get("lahm")], inc="a")
        free["lchm"] = edh
        t3 = r.op("tensor", lambda e: e.transpose(
            pb[0:128, 5 * 512:5 * 512 + 4],
            lahm[:, :], idnf[0:4, 0:4]), waits=[edh, bank_free[5]], inc="t")
        ed = r.op("vector", lambda e: e.tensor_copy(
            dtm[:, :], pb[0:128, 5 * 512:5 * 512 + 4]),
            waits=[t3, free.get("dtm")], inc="v")
        t4 = r.op("tensor", lambda e: e.transpose(
            pb[0:128, 5 * 512 + 128:5 * 512 + 132], wuhm[:, :],
            idnf[0:4, 0:4]), waits=[ewu2, ed], inc="t")
        ewt = r.op("vector", lambda e: e.tensor_copy(
            wutm[:, :], pb[0:128, 5 * 512 + 128:5 * 512 + 132]),
            waits=[t4, free.get("wutm")], inc="v")
        bank_free[5] = ewt
        free["wuhm"] = t4
        elr = r.dma("gpsimd", out=lcrow[0:1, 0:512].rearrange(
            "o (h t) -> o h t", h=4), in_=lchm[:, :],
            waits=[esc, free.get("lcrow")], inc=sem_lr)
        eng = r.op("vector", lambda e: e.tensor_scalar_mul(
            lcrow[0:1, 512:1024], lcrow[0:1, 0:512], -1.0),
            waits=[elr], inc="v")
        free["lcrow"] = eng
        for h in range(4):
            bq = 6
            tq_ = r.op("tensor", lambda e, c=c, h=h: e.transpose(
                ptr[0:64, bq * 1024:bq * 1024 + 128],
                qtm[:, c * 256 + h * 64:c * 256 + (h + 1) * 64], idn[:, :]),
                waits=[phA[0], idn_ev, bank_free[bq]], inc="t")
            eqf = r.op("scalar", lambda e, h=h: e.copy(
                qfm_[:, h * 128:(h + 1) * 128],
                ptr[0:64, bq * 1024:bq * 1024 + 128]),
                waits=[tq_, free.get(f"qfm{h}")], inc="a")
            evu = r.op("vector", lambda e, c=c, h=h: e.tensor_scalar(
                out=qdc[:, h * 64:(h + 1) * 64],
                in0=vtm[:, c * 256 + h * 64:c * 256 + (h + 1) * 64],
                scalar1=utm[:, c * 4 + h:c * 4 + h + 1], scalar2=None,
                op0=ALU.mult),
                waits=[phA[2], phA[3], free.get(f"qdc{h}")], inc="v")
            tv_ = r.op("tensor", lambda e, c=c, h=h: e.transpose(
                ptr[0:64, bq * 1024 + 128:bq * 1024 + 256],
                qdc[:, h * 64:(h + 1) * 64], idn[:, :]),
                waits=[evu, eqf], inc="t")
            evf = r.op("scalar", lambda e, h=h: e.copy(
                vfm_[:, h * 128:(h + 1) * 128],
                ptr[0:64, bq * 1024 + 128:bq * 1024 + 256]),
                waits=[tv_, free.get(f"vfm{h}")], inc="a")
            bank_free[bq] = evf
            m1 = r.op("tensor", lambda e, h=h: e.matmul(
                pbank(7, 128), lcrow[0:1, 512 + h * 128:512 + (h + 1) * 128],
                onesf[:, :], start=True, stop=False),
                waits=[eng, ev_onf, bank_free[7]], inc="t")
            m2 = r.op("tensor", lambda e, h=h: e.matmul(
                pbank(7, 128), onesf[:, :],
                lcrow[0:1, h * 128:(h + 1) * 128], start=False, stop=True),
                waits=[], inc="t")
            ema = r.op("vector", lambda e: e.tensor_add(
                dmt[:, :], pbank(7, 128), trim[:, :]),
                waits=[m2, trim_ev, free.get("dmt")], inc="v")
            bank_free[7] = ema
            eex = r.op("scalar", lambda e: e.activation(
                out=gwt[:, :].bitcast(F32)[:, 0:64] if False else dmt[:, :],
                in_=dmt[:, :], func=AF.Exp), waits=[ema], inc="a")
            free["dmt"] = eex
            m3 = r.op("tensor", lambda e, h=h: e.matmul(
                pbank(6, 128), vfm_[:, h * 128:(h + 1) * 128],
                qfm_[:, h * 128:(h + 1) * 128], start=True, stop=True),
                waits=[evf, bank_free[6]], inc="t")
            egw = r.op("vector", lambda e: e.tensor_mul(
                gwt[:, :], pbank(6, 128), dmt[:, :]),
                waits=[m3, eex, free.get("gwt")], inc="v")
            bank_free[6] = egw
            m4 = r.op("tensor", lambda e, c=c, h=h: e.matmul(
                pbank(4, 64), gwt[:, :],
                ktm[:, c * 256 + h * 64:c * 256 + (h + 1) * 64],
                start=True, stop=True),
                waits=[egw, phA[1], bank_free[4]], inc="t")
            free["gwt"] = m4
            ecv = r.op("vector", lambda e, Scur=Scur, h=h: e.tensor_copy(
                Sbf[:, h * 64:(h + 1) * 64], Sst[Scur][:, h * 64:(h + 1) * 64]),
                waits=[S_ev, free.get(f"Sbf{h}")], inc="v")
            m5 = r.op("tensor", lambda e, h=h: e.matmul(
                pbank(3, 64), qfm_[:, h * 128:(h + 1) * 128],
                Sbf[:, h * 64:(h + 1) * 64], start=True, stop=True),
                waits=[ecv, eqf, bank_free[3]], inc="t")
            free[f"Sbf{h}"] = m5
            free[f"qfm{h}"] = m5
            ysl = c % 2
            ey1 = r.op("vector", lambda e, h=h, ysl=ysl: e.tensor_copy(
                ysc[ysl][:, h * 64:(h + 1) * 64],
                pbank(4, 64)), waits=[m4, free.get(f"ysc{ysl}")
                                      if h == 0 else None], inc="v")
            bank_free[4] = ey1
            ey2 = r.op("vector", lambda e, h=h, ysl=ysl: e.scalar_tensor_tensor(
                out=ysc[ysl][:, h * 64:(h + 1) * 64],
                in0=pbank(3, 64), scalar=dtm[:, h:h + 1],
                in1=ysc[ysl][:, h * 64:(h + 1) * 64],
                op0=ALU.mult, op1=ALU.add), waits=[ey1, ed, m5], inc="v")
            bank_free[3] = ey2
            ysl_ev = ey2
            m6 = r.op("tensor", lambda e, h=h: e.matmul(
                pb[0:64, 5 * 512 + 256:5 * 512 + 257],
                onesf[:, 0:64], lcrow[0:1, h * 128 + 127:h * 128 + 128],
                start=True, stop=True), waits=[elr, bank_free[5]], inc="t")
            edc = r.op("scalar", lambda e, h=h: e.activation(
                out=dcol[:, h:h + 1], in_=pb[0:64, 5 * 512 + 256:5 * 512 + 257],
                func=AF.Exp), waits=[m6, free.get("dcol")], inc="a")
            bank_free[5] = edc
            evw = r.op("vector", lambda e, c=c, h=h: e.tensor_scalar(
                out=qdc[:, h * 64:(h + 1) * 64],
                in0=qdc[:, h * 64:(h + 1) * 64],
                scalar1=wutm[:, h:h + 1], scalar2=None, op0=ALU.mult),
                waits=[ewt, tv_], inc="v")
            m7 = r.op("tensor", lambda e, c=c, h=h: e.matmul(
                pb[0:64, 2 * 512:2 * 512 + 64], qdc[:, h * 64:(h + 1) * 64],
                ktm[:, c * 256 + h * 64:c * 256 + (h + 1) * 64],
                start=True, stop=True), waits=[evw, bank_free[2]], inc="t")
            free[f"qdc{h}"] = m7
            eS2 = r.op("vector", lambda e, Scur=Scur, h=h: e.scalar_tensor_tensor(
                out=Sst[1 - Scur][:, h * 64:(h + 1) * 64],
                in0=Sst[Scur][:, h * 64:(h + 1) * 64],
                scalar=dcol[:, h:h + 1],
                in1=pb[0:64, 2 * 512:2 * 512 + 64],
                op0=ALU.mult, op1=ALU.add),
                waits=[m7, edc, ecv], inc="v")
            bank_free[2] = eS2
            free["dcol"] = eS2
            S_ev_new = eS2
        free["dtm"] = ey2
        free["wutm"] = m7
        for h in range(4):
            free[f"vfm{h}"] = m3
        S_ev = S_ev_new
        Scur = 1 - Scur
        free["lahm"] = t3
        est = r.dma("sync", out=y_d[c * 128:(c + 1) * 128, :],
                    in_=ysc[c % 2][:, :], waits=[ysl_ev], inc=sem_ys[c % 2])
        free[f"ysc{c % 2}"] = est
        ylast[c % 2] = est
    return ylast



def emit_s7b(r, xt, x_ready, ycat_d, woT_d, tag="s7b", idn=None,
             idn_ev=None, fused=None):
    if idn is None:
        idn, idn_ev = make_id128(r, BF16, tag + "_id")
    pb = r.psum(tag + "_pb", [128, 4096], F32)
    pbank = lambda b, w=512: pb[:, b * 512:b * 512 + w]
    ptr = pb.bitcast(BF16)
    bank_free = {b: None for b in range(8)}
    wo = r.sbuf(tag + "_wo", [128, 8 * 1024], BF16)
    e_w = r.dma("gpsimd", out=wo[:, :].rearrange("p (c f) -> p c f", c=8),
                in_=woT_d.rearrange("(c p) f -> p c f", p=128), inc="dG")
    yld = [r.sbuf(tag + f"_yl{p}", [128, 1024], BF16) for p in range(2)]
    yfm = [r.sbuf(tag + f"_yf{p}", [128, 1024], BF16) for p in range(2)]
    sem_yl = [r.sem(tag + f"_yl{p}") for p in range(2)]
    free = {}
    last = None
    for t in range(16):
        sl = t % 2
        if fused is None:
            eyl = r.dma("sync", out=yld[sl][:, :],
                        in_=ycat_d[t * 128:(t + 1) * 128, :],
                        waits=[free.get(f"yld{sl}")], inc=sem_yl[sl])
        else:
            for rrk in range(4):
                eyl = r.op("gpsimd", lambda e, sl=sl, t=t, rrk=rrk:
                           e.indirect_dma_start(
                               out=yld[sl][:, rrk * 256:(rrk + 1) * 256],
                               out_offset=None, in_=fused["ygall_d"],
                               in_offset=IndirectOffsetOnAxis(
                                   fused["yidx"][:, t * 4 + rrk:
                                                 t * 4 + rrk + 1], 0)),
                           waits=([free.get(f"yld{sl}"), fused["at_ev"],
                                   fused["yidx_ev"]] if rrk == 0 else []),
                           inc=sem_yl[sl], inc_k=16, dma=True)
        atr = None
        for j in range(8):
            b = 6 + (j % 2)
            tr = r.op("tensor", lambda e, b=b, j=j, sl=sl: e.transpose(
                ptr[:, b * 1024:b * 1024 + 128],
                yld[sl][:, j * 128:(j + 1) * 128], idn[:, :]),
                waits=[eyl, idn_ev, bank_free[b], free.get(f"yfm{sl}")],
                inc="t")
            ec = r.op("scalar", lambda e, b=b, j=j, sl=sl: e.copy(
                yfm[sl][:, j * 128:(j + 1) * 128],
                ptr[:, b * 1024:b * 1024 + 128]), waits=[tr], inc="a")
            bank_free[b] = ec
            atr = ec
        free[f"yld{sl}"] = tr
        for eh in range(2):
            yb = eh
            mm = None
            for c in range(8):
                w0 = [atr, bank_free[yb], e_w] if c == 0 else []
                mm = r.op("tensor", lambda e, yb=yb, c=c, eh=eh, sl=sl:
                          e.matmul(pbank(yb),
                                   yfm[sl][:, c * 128:(c + 1) * 128],
                                   wo[:, c * 1024 + eh * 512:
                                      c * 1024 + (eh + 1) * 512],
                                   start=(c == 0), stop=(c == 7)),
                          waits=w0, inc="t" if c == 7 else None)
            xsl = xt[:, t * 1024 + eh * 512:t * 1024 + (eh + 1) * 512]
            ry = r.op("vector", lambda e, yb=yb, xsl=xsl: e.tensor_add(
                xsl, xsl, pbank(yb)), waits=[mm, x_ready], inc="v")
            bank_free[yb] = ry
            last = ry
        free[f"yfm{sl}"] = mm
    return last



GROUPS = [[0, 1, 2, 3], [4, 5, 6, 7]]


def emit_lnq(r, xt, xnq_d, tag, qsend_d=None, x_ready=None, xb=None):
    ln = LN(r, tag + "_ln", 4)
    if xb is None:
        xb = [r.sbuf(tag + f"_xb{p}", [128, 1024], BF16) for p in range(2)]
    sem_st = [r.sem("lnq_st0"), r.sem("lnq_st1")]
    free = {}
    last = [None, None]
    for t in range(16):
        sl = t % 2
        lev = ln.tile(t % 4, xt[:, t * 1024:(t + 1) * 1024], xb[sl][:, :],
                      x_ready=[x_ready, free.get(f"xb{sl}")])
        ev = r.dma("scalar", out=xnq_d[t * 128:(t + 1) * 128, :],
                   in_=xb[sl][:, :], waits=[lev], inc=sem_st[sl])
        if qsend_d is not None:
            off = (t % 4) * 512 + (t // 4) * 128
            ev = r.dma("scalar", out=qsend_d[off:off + 128, :],
                       in_=xb[sl][:, :], waits=[lev], inc=sem_st[sl])
        free[f"xb{sl}"] = ev
        last[sl] = ev
    return last


def emit_cc(r, kind, ins, outs, waits, op=ALU.bypass):
    return r.op("gpsimd", lambda e: e.collective_compute(
        kind, op, replica_groups=GROUPS, ins=[a.opt() for a in ins],
        outs=[a.opt() for a in outs]), waits=waits, inc="cc")


def build_fused():
    nc = bass.Bass("TRN2", target_bir_lowering=False)
    ap = lambda name, shape, dt=F32: nc.dram_tensor(
        name, shape, dt, kind="ExternalInput").ap()
    xh = ap("xh", [2304, D])
    s1_wqkvT = ap("s1_wqkvT", [1024, 3072])
    s1_woT = ap("s1_woT", [1024, 1024])
    s1_bout = ap("s1_bout", [1, 1024])
    s1_csk = ap("s1_csk", [2304, 32]); s1_snk = ap("s1_snk", [2304, 32])
    s1_csq = ap("s1_csq", [2048, 32]); s1_snq = ap("s1_snq", [2048, 32])
    s1_masks = ap("s1_masks", [3, 128, 384])
    k0b = ap("k0b", [32, 128, 16, 128], U8)
    k1b = ap("k1b", [32, 128, 32, 128], U8)
    k2r = ap("k2r", [4096, 1024], U8)
    svec = ap("svec", [1, 3])
    s3_wqkv = ap("s3_wqkv", [1024, 3072])
    s3_wproj = ap("s3_wproj", [1024, 1024])
    s3_bproj = ap("s3_bproj", [1, 1024])
    s3_qgw = ap("s3_qgw", [1, 1024]); s3_kgw = ap("s3_kgw", [1, 1024])
    s3_cq = ap("s3_cq", [2048, 512]); s3_sq = ap("s3_sq", [2048, 512])
    s3_ck = ap("s3_ck", [T, 512]); s3_sk = ap("s3_sk", [T, 512])
    s3_masks = ap("s3_masks", [16, 128, 1024], BF16)
    s5_wq = ap("s5_wq", [1024, 1024]); s5_wk = ap("s5_wk", [1024, 512])
    s5_wv = ap("s5_wv", [1024, 512]); s5_wo = ap("s5_wo", [1024, 1024])
    s5_cq = ap("s5_cq", [2048, 512]); s5_sq = ap("s5_sq", [2048, 512])
    s5_ck = ap("s5_ck", [T, 256]); s5_sk = ap("s5_sk", [T, 256])
    s6_rwt = ap("s6_rwt", [1024, 8]); s6_rbt = ap("s6_rbt", [1, 8])
    s6_w1b = ap("s6_w1b", [8, 32, 128, 8, 128])
    s6_b1 = ap("s6_b1", [128, 256])
    s6_w2 = ap("s6_w2", [8, 4096, 1024])
    s6_b2 = ap("s6_b2", [1, 8192])
    s7_wqT = ap("s7_wqT", [1024, 256]); s7_wkT = ap("s7_wkT", [1024, 256])
    s7_wvT = ap("s7_wvT", [1024, 256]); s7_woT = ap("s7_woT", [1024, 1024])
    s7_wg = ap("s7_wg", [1, 4]); s7_bg = ap("s7_bg", [1, 4])
    s7_wl = ap("s7_wl", [1, 4]); s7_bl = ap("s7_bl", [1, 4])
    xo = nc.dram_tensor("xo", [TL, D], F32, kind="ExternalOutput").ap()

    qidx_d = ap("qidx", [128, 16], mybir.dt.int32)
    yidx_d = ap("yidx", [128, 16], mybir.dt.int32)
    y7idx_d = ap("y7idx", [128, 64], mybir.dt.int32)
    idr = lambda name, shape, dt=BF16: nc.dram_tensor(name, shape, dt).ap()
    xnq_d = idr("xnq_d", [TL, D])
    xnf_d = idr("xnf_d", [T, D])
    ybuf_d = idr("ybuf_d", [TL, D])
    ygall_d = idr("ygall_d", [4 * TL, D])
    y7_d = idr("y7_d", [T, 256])
    y7g_d = idr("y7g_d", [4 * T, 256])

    r = Rec(nc)
    xt = r.sbuf("xt", [128, 16 * D], F32)
    qidx = r.sbuf("qidx", [128, 16], mybir.dt.int32)
    yidx = r.sbuf("yidx", [128, 16], mybir.dt.int32)
    y7idx = r.sbuf("y7idx", [128, 64], mybir.dt.int32)
    qidx_ev = r.dma("sync", out=qidx[:, :], in_=qidx_d, inc="dX")
    yidx_ev = r.dma("sync", out=yidx[:, :], in_=yidx_d, inc="dX")
    y7idx_ev = r.dma("sync", out=y7idx[:, :], in_=y7idx_d, inc="dX")
    x_ready = r.dma("sync", out=xt[:, :].rearrange("p (t d) -> p t d", d=D),
                    in_=xh[128:128 + TL, :].rearrange("(t p) d -> p t d",
                                                      p=128), inc="dX")

    r.scope_begin()
    halo = r.sbuf("halo", [128, 2 * D], F32)
    h_ready = None
    for i, off in enumerate([0, 2176]):
        h_ready = r.dma("sync", out=halo[:, i * D:(i + 1) * D],
                        in_=xh[off:off + 128, :], inc="dX")

    def xts1(i):
        if i == 0:
            return halo[:, 0:D]
        if i == 17:
            return halo[:, D:2 * D]
        return xt[:, (i - 1) * D:i * D]

    emit_s1(r, xt, h_ready, s1_wqkvT, s1_woT, s1_bout, s1_csk, s1_snk,
            s1_csq, s1_snq, s1_masks, tag="s1", xts=xts1)
    r.barrier(); r.scope_end()
    r.scope_begin()
    emit_s2(r, xt, None, k0b, k1b, k2r, svec, tag="s2")
    r.barrier(); r.scope_end()
    r.scope_begin()
    xb3 = [r.sbuf(f"xb3{p}", [128, 1024], BF16) for p in range(2)]
    lnq_ev = emit_lnq(r, xt, xnq_d, "lq3", xb=xb3)
    ag = emit_cc(r, "AllGather", [xnq_d], [xnf_d], lnq_ev)
    emit_s3(r, xt, None, None, None, s3_wqkv, s3_wproj, s3_bproj,
            s3_qgw, s3_kgw, s3_cq, s3_sq, s3_ck, s3_sk, s3_masks, tag="s3",
            fused=dict(xnf_d=xnf_d, ybuf_d=ybuf_d, ag_ev=ag,
                       qidx=qidx, qidx_ev=qidx_ev))
    aty = emit_cc(r, "AllGather", [ybuf_d], [ygall_d],
                  [r.last_ev.get("s3_ql0"), r.last_ev.get("s3_ql1")])
    ybl = xb3
    sem_yb = [r.sem("yb0"), r.sem("yb1")]
    yfree = [r.last_ev.get("lnq_st0"), r.last_ev.get("lnq_st1")]
    for i in range(16):
        sl = i % 2
        eld = r.op("gpsimd", lambda e, sl=sl, i=i: e.indirect_dma_start(
            out=ybl[sl][:, :], out_offset=None, in_=ygall_d,
            in_offset=IndirectOffsetOnAxis(yidx[:, i:i + 1], 0)),
            waits=[aty, yfree[sl], yidx_ev], inc=sem_yb[sl], inc_k=16,
            dma=True)
        ead = r.op("vector", lambda e, i=i, sl=sl: e.tensor_add(
            xt[:, i * D:(i + 1) * D], xt[:, i * D:(i + 1) * D],
            ybl[sl][:, :]), waits=[eld], inc="v")
        yfree[sl] = ead
    r.barrier(); r.scope_end()
    r.scope_begin()
    emit_s2(r, xt, None, k0b, k1b, k2r, svec, tag="s2")
    r.barrier(); r.scope_end()
    r.scope_begin()
    lnq_ev = emit_lnq(r, xt, xnq_d, "lq5")
    ag = emit_cc(r, "AllGather", [xnq_d], [xnf_d], lnq_ev)
    emit_s5(r, xt, None, None, s5_wq, s5_wk, s5_wv, s5_wo,
            s5_cq, s5_sq, s5_ck, s5_sk, tag="s5",
            fused=dict(xnf_d=xnf_d, xnq_d=xnq_d, ag_ev=ag,
                       lnq_ev=lnq_ev[1], lnq_ev0=lnq_ev[0]))
    r.barrier(); r.scope_end()
    r.scope_begin()
    emit_s6(r, xt, None, s6_rwt, s6_rbt, s6_w1b, s6_b1, s6_w2, s6_b2,
            tag="s6")
    r.barrier(); r.scope_end()
    r.scope_begin()
    lnq_ev = emit_lnq(r, xt, xnq_d, "lq7")
    ag = emit_cc(r, "AllGather", [xnq_d], [xnf_d], lnq_ev)
    ey7 = emit_s7a(r, None, s7_wqT, s7_wkT, s7_wvT, s7_wg, s7_bg, s7_wl,
                   s7_bl, y7_d, tag="s7",
                   fused=dict(xnf_d=xnf_d, ag_ev=ag))
    at7 = emit_cc(r, "AllGather", [y7_d], [y7g_d], ey7)
    r.barrier(); r.scope_end()
    r.scope_begin()
    emit_s7b(r, xt, None, None, s7_woT, tag="s7b",
             fused=dict(ygall_d=y7g_d, yidx=y7idx, yidx_ev=y7idx_ev,
                        at_ev=at7))
    r.barrier(); r.scope_end()
    r.scope_begin()
    done = emit_s6(r, xt, None, s6_rwt, s6_rbt, s6_w1b, s6_b1, s6_w2, s6_b2,
                   tag="s6")
    st = r.dma("sync", out=xo.rearrange("(t p) d -> p t d", p=128),
               in_=xt[:, :].rearrange("p (t d) -> p t d", d=D),
               waits=[done], inc="dO")
    r.wait("sync", st)
    r.scope_end()
    r.emit()
    return nc


def fused_in_map(inp, b, q, statics):
    s = statics
    X = inp["L0_attn_a_x"]
    xpad = s["xpad"]
    csk, snk, csq, snq, masks1 = s["s1_tabs"][q]
    rows3, cq3, sq3, ck3, sk3, masks3 = s["s3_tabs"][q]
    cq5, sq5, ck5, sk5 = s["s5_tabs"][q]
    p = np.arange(128, dtype=np.int32)
    qidx = np.stack([(4 * j + q) * 128 + p for j in range(16)], 1)
    yidx = np.stack([(np.int32(i % 4) * 2048 + (4 * q + i // 4) * 128 + p)
                     for i in range(16)], 1)
    y7idx = np.stack([rrk * 8192 + q * 2048 + t * 128 + p
                      for t in range(16) for rrk in range(4)], 1)
    cols7 = np.arange(q * 256, (q + 1) * 256)
    return {
        "xh": np.ascontiguousarray(xpad[b, q * TL:q * TL + 2304]),
        "s1_wqkvT": s["s1_wqkvT"], "s1_woT": s["s1_woT"],
        "s1_bout": s["s1_bout"], "s1_csk": csk, "s1_snk": snk,
        "s1_csq": csq, "s1_snq": snq, "s1_masks": masks1,
        "k0b": s["k0b"], "k1b": s["k1b"], "k2r": s["k2r"], "svec": s["svec"],
        "s3_wqkv": s["s3_wqkv"], "s3_wproj": s["s3_wproj"],
        "s3_bproj": s["s3_bproj"], "s3_qgw": s["s3_qgw"],
        "s3_kgw": s["s3_kgw"], "s3_cq": cq3, "s3_sq": sq3,
        "s3_ck": ck3, "s3_sk": sk3, "s3_masks": masks3,
        "s5_wq": s["s5_wq"], "s5_wk": s["s5_wk"], "s5_wv": s["s5_wv"],
        "s5_wo": s["s5_wo"], "s5_cq": cq5, "s5_sq": sq5,
        "s5_ck": ck5, "s5_sk": sk5,
        "s6_rwt": s["s6_rwt"], "s6_rbt": s["s6_rbt"], "s6_w1b": s["s6_w1b"],
        "s6_b1": s["s6_b1"], "s6_w2": s["s6_w2"], "s6_b2": s["s6_b2"],
        "s7_wqT": np.ascontiguousarray(s["s7_wqT"][:, cols7]),
        "s7_wkT": np.ascontiguousarray(s["s7_wkT"][:, cols7]),
        "s7_wvT": np.ascontiguousarray(s["s7_wvT"][:, cols7]),
        "s7_woT": s["s7_woT"],
        "s7_wg": s["s7_wg"][None, q * 4:(q + 1) * 4],
        "s7_bg": s["s7_bg"][None, q * 4:(q + 1) * 4],
        "s7_wl": s["s7_wl"][None, q * 4:(q + 1) * 4],
        "s7_bl": s["s7_bl"][None, q * 4:(q + 1) * 4],
        "qidx": np.ascontiguousarray(qidx.astype(np.int32)),
        "yidx": np.ascontiguousarray(yidx.astype(np.int32)),
        "y7idx": np.ascontiguousarray(y7idx.astype(np.int32)),
    }


def _s5_tabs_global(q):
    pos = np.arange(T) + 1.0
    thq = 1.0 / 10000.0 ** (np.arange(0, 1024, 2) / 1024.0)
    thk = 1.0 / 10000.0 ** (np.arange(0, 512, 2) / 512.0)
    angq = (pos[q * TL:(q + 1) * TL, None] * thq[None, :]).astype(np.float32)
    angk = (pos[:, None] * thk[None, :]).astype(np.float32)
    cq = (np.cos(angq.astype(np.float64)) * 0.5).astype(np.float32)
    sq = (np.sin(angq.astype(np.float64)) * 0.5).astype(np.float32)
    ck = np.cos(angk.astype(np.float64)).astype(np.float32)
    sk = np.sin(angk.astype(np.float64)).astype(np.float32)
    return cq, sq, ck, sk


_NPATCH = 220

def _tabs_s1(q):
    pos_k = q * TL - 128 + np.arange(2304)
    pos_q = q * TL + np.arange(TL)
    inv = (1.0 / 10000.0 ** (np.arange(0, 64, 2, dtype=np.float32) / 64.0)
           ).astype(np.float32)
    fk = (np.clip(pos_k, 0, T - 1)[:, None].astype(np.float32) * inv[None, :]
          ).astype(np.float32)
    fq = (pos_q[:, None].astype(np.float32) * inv[None, :]).astype(np.float32)
    csk = np.cos(fk.astype(np.float64)).astype(np.float32)
    snk = np.sin(fk.astype(np.float64)).astype(np.float32)
    csq = (np.cos(fq.astype(np.float64)) * 0.125).astype(np.float32)
    snq = (np.sin(fq.astype(np.float64)) * 0.125).astype(np.float32)
    tq = np.arange(128)[:, None]
    tk = np.arange(384)[None, :]

    def bandmask(g):
        kglob = g * 128 + tk - 128
        valid = ((tk >= tq + 1) & (tk <= tq + 256) & (kglob >= 0)
                 & (kglob < T))
        return np.where(valid, 0.0, -1e30).astype(np.float32)

    masks = np.stack([bandmask(q * 16 + 0), bandmask(q * 16 + 1),
                      bandmask(q * 16 + 15)])
    return csk, snk, csq, snq, masks


def _tabs_s3(rr):
    tiles = 4 * np.arange(16) + rr
    rows = (tiles[:, None] * 128 + np.arange(128)[None, :]).reshape(-1)
    th = 10000.0 ** (-np.arange(32) / 32.0)
    angq = rows.astype(np.float64)[:, None] * np.tile(th, 16)[None, :]
    angk = np.arange(T, dtype=np.float64)[:, None] * np.tile(th, 16)[None, :]
    cqv = (np.cos(angq) * 0.125).astype(np.float32)
    sqv = (np.sin(angq) * 0.125).astype(np.float32)
    ckv = np.cos(angk).astype(np.float32)
    skv = np.sin(angk).astype(np.float32)
    import ml_dtypes
    masks = np.zeros((16, 128, 1024), np.float32)
    for j in range(16):
        g = 4 * j + rr
        cnt = min((j + 2) * 512, T)
        qpos = g * 128 + np.arange(128)
        kpos = cnt - 1024 + np.arange(1024)
        bad = (kpos[None, :] // _NPATCH) > (qpos[:, None] // _NPATCH)
        masks[j] = np.where(bad, -1e30, 0.0)
    return rows, cqv, sqv, ckv, skv, masks.astype(ml_dtypes.bfloat16)


def _tabs_s5(q):
    rot = np.roll(np.arange(T), -q * TL)
    pos = rot + 1.0
    thq = 1.0 / 10000.0 ** (np.arange(0, 1024, 2) / 1024.0)
    thk = 1.0 / 10000.0 ** (np.arange(0, 512, 2) / 512.0)
    angq = (pos[:TL, None] * thq[None, :]).astype(np.float32)
    angk = (pos[:, None] * thk[None, :]).astype(np.float32)
    cq = (np.cos(angq.astype(np.float64)) * 0.5).astype(np.float32)
    sq = (np.sin(angq.astype(np.float64)) * 0.5).astype(np.float32)
    ck = np.cos(angk.astype(np.float64)).astype(np.float32)
    sk = np.sin(angk.astype(np.float64)).astype(np.float32)
    return rot, cq, sq, ck, sk




def fused_statics(inp):
    X = np.ascontiguousarray(inp["L0_attn_a_x"].astype(np.float32))
    B = X.shape[0]
    xpad = np.zeros((B, T + 256, D), np.float32)
    xpad[:, 128:128 + T] = X
    W1 = np.asarray(inp["L1_ffn_W1"], np.float32)
    b1 = np.asarray(inp["L1_ffn_b1"], np.float32)
    return {
        "xpad": xpad,
        "s1_wqkvT": np.ascontiguousarray(inp["L0_attn_a_w_qkv"].T),
        "s1_woT": np.ascontiguousarray(inp["L0_attn_a_w_out"].T),
        "s1_bout": np.asarray(inp["L0_attn_a_b_out"], np.float32)[None],
        "s1_tabs": [_tabs_s1(q) for q in range(4)],
        "k0b": blockify_lhsT(np.asarray(inp["L0_ffn_k0"], np.uint8)),
        "k1b": blockify_lhsT(np.asarray(inp["L0_ffn_k1"], np.uint8)),
        "k2r": np.ascontiguousarray(np.asarray(inp["L0_ffn_k2"], np.uint8)),
        "svec": np.array([[np.float32(inp["L0_ffn_s0"]),
                           np.float32(inp["L0_ffn_s1"]),
                           np.float32(inp["L0_ffn_s2"])]], np.float32),
        "s3_wqkv": np.ascontiguousarray(inp["L0_attn_b_Wqkv"], np.float32),
        "s3_wproj": np.ascontiguousarray(inp["L0_attn_b_Wproj"], np.float32),
        "s3_bproj": np.asarray(inp["L0_attn_b_bproj"], np.float32)[None],
        "s3_qgw": np.ascontiguousarray(
            np.tile(np.asarray(inp["L0_attn_b_qn_w"], np.float32), 16))[None],
        "s3_kgw": np.ascontiguousarray(
            np.tile(np.asarray(inp["L0_attn_b_kn_w"], np.float32), 16))[None],
        "s3_tabs": [_tabs_s3(rr) for rr in range(4)],
        "s5_wq": np.ascontiguousarray(inp["L1_attn_a_Wq"], np.float32),
        "s5_wk": np.ascontiguousarray(inp["L1_attn_a_Wk"], np.float32),
        "s5_wv": np.ascontiguousarray(inp["L1_attn_a_Wv"], np.float32),
        "s5_wo": np.ascontiguousarray(inp["L1_attn_a_Wo"], np.float32),
        "s5_tabs": [_s5_tabs_global(q) for q in range(4)],
        "s6_rwt": np.ascontiguousarray(inp["L1_ffn_router_W"], np.float32),
        "s6_rbt": np.asarray(inp["L1_ffn_router_b"], np.float32)[None],
        "s6_w1b": np.ascontiguousarray(np.stack(
            [blockify_lhsT(W1[e]) for e in range(8)])),
        "s6_b1": np.ascontiguousarray(
            b1.reshape(8, 32, 128).transpose(2, 0, 1).reshape(128, 256)),
        "s6_w2": np.ascontiguousarray(inp["L1_ffn_W2"], np.float32),
        "s6_b2": np.ascontiguousarray(
            np.asarray(inp["L1_ffn_b2"], np.float32).reshape(1, -1)),
        "s7_wqT": np.ascontiguousarray(inp["L1_attn_b_Wq"].T),
        "s7_wkT": np.ascontiguousarray(inp["L1_attn_b_Wk"].T),
        "s7_wvT": np.ascontiguousarray(inp["L1_attn_b_Wv"].T),
        "s7_woT": np.ascontiguousarray(inp["L1_attn_b_Wo"].T),
        "s7_wg": np.asarray(inp["L1_attn_b_W_gamma"], np.float32),
        "s7_bg": np.asarray(inp["L1_attn_b_b_gamma"], np.float32),
        "s7_wl": np.asarray(inp["L1_attn_b_W_lambda"], np.float32),
        "s7_bl": np.asarray(inp["L1_attn_b_b_lambda"], np.float32),
    }


def kernel(**inp):
    from concourse.bass_utils import run_bass_kernel_spmd
    statics = fused_statics(inp)
    X = np.ascontiguousarray(inp["L0_attn_a_x"].astype(np.float32))
    nc = build_fused()
    in_maps = [fused_in_map(inp, c // 4, c % 4, statics) for c in range(8)]
    res = run_bass_kernel_spmd(nc, in_maps, core_ids=list(range(8)))
    out = np.empty_like(X)
    for c in range(8):
        out[c // 4, (c % 4) * TL:(c % 4 + 1) * TL] = res.results[c]["xo"]
    return out
